# version 07
# baseline (speedup 1.0000x reference)
import numpy as np
import contextlib
import concourse.bass as bass
import concourse.mybir as mybir
from concourse.bass import IndirectOffsetOnAxis

F32 = mybir.dt.float32
BF16 = mybir.dt.bfloat16
U32 = mybir.dt.uint32
AF = mybir.ActivationFunctionType
ALU = mybir.AluOpType
AX = mybir.AxisListType

H = 16
D = 128
DM = 2048
KC = 4
NCORE = 8
HPC = 2
SCALE = float(D) ** -0.5
LN_EPS = 1e-5
L = 32
HS = 2816
HHALF = 1408
SHS = 5632
SHH = 2816
NE = 8
BIGNEG = 30000.0


class Sem:
    def __init__(self, nc, name, stack):
        self.h = stack.enter_context(nc.semaphore(name))
        self.n = 0

    def inc(self, instr, k=1):
        instr.then_inc(self.h, k)
        self.n += k
        return self.n


class Builder:
    def __init__(self, T=16384, debug=(), stop_after='full'):
        self.T = T
        self.TSL = T // NCORE
        self.SEG = min(1024, self.TSL)
        assert self.TSL % self.SEG == 0 and self.SEG % L == 0
        self.NSEG = T // self.SEG
        self.NCS = self.SEG // L
        self.NBLK = self.SEG // 128
        self.NLB = self.TSL // 128
        self.debug = set(debug)
        self.stop_after = stop_after
        self.dbgt = {}
        self.nc = bass.Bass("TRN2", target_bir_lowering=False)
        self._stack = contextlib.ExitStack()

    def sb(self, name, shape, dtype, stack=None):
        return (stack or self._stack).enter_context(self.nc.sbuf_tensor(name, shape, dtype))

    def ps(self, name, shape, dtype, stack=None):
        return (stack or self._stack).enter_context(self.nc.psum_tensor(name, shape, dtype))

    def dbg(self, name, shape, dtype=F32):
        if name in self.debug:
            self.dbgt[name] = self.nc.dram_tensor("dbg_" + name, shape, dtype, kind="ExternalOutput").ap()

    def barrier(self):
        nc = self.nc
        sems = [self.s_ld, self.s_w, self.s_pe, self.s_dv, self.s_se, self.s_st, self.s_gp, self.s_cc]
        for eng in (nc.tensor, nc.vector, nc.scalar, nc.sync, nc.gpsimd):
            for s in sems:
                if s.n > 0:
                    eng.wait_ge(s.h, s.n)

    def build(self):
        nc = self.nc
        T, TSL = self.T, self.TSL
        dr = lambda n, s, d=F32: nc.dram_tensor(n, s, d, kind="ExternalInput").ap()
        di = lambda n, s, d=F32: nc.dram_tensor(n, s, d, kind="Internal").ap()

        self.x_sl = dr("x_sl", [TSL, DM])
        self.wq_s = dr("wq_s", [DM, HPC * D]); self.wk_s = dr("wk_s", [DM, HPC * D])
        self.wv_s = dr("wv_s", [DM, HPC * D]); self.wa_s = dr("wa_s", [DM, HPC * D])
        self.wg_s = dr("wg_s", [DM, HPC * D])
        self.ba_s = dr("ba_s", [HPC * D]); self.wb_full = dr("wb_full", [DM, H]); self.bb_in = dr("bb", [H])
        self.qcw = dr("qcw_s", [HPC * D, KC]); self.kcw = dr("kcw_s", [HPC * D, KC]); self.vcw = dr("vcw_s", [HPC * D, KC])
        self.qcb = dr("qcb_s", [HPC * D]); self.kcb = dr("kcb_s", [HPC * D]); self.vcb = dr("vcb_s", [HPC * D])
        self.ln_g = dr("ln_g", [D]); self.ln_b = dr("ln_b", [D])
        self.wo = dr("wo", [DM, DM])
        self.gate_wT = dr("gate_wT", [DM, NE]); self.gate_b = dr("gate_b", [NE])
        self.shared_gwT = dr("shared_gwT", [DM, SHS]); self.shared_dwT = dr("shared_dwT", [SHH, DM])
        self.up_wT = dr("up_wT", [NE, DM, HS]); self.down_wT = dr("down_wT", [NE, HHALF, DM])
        self.ident_bf = dr("ident_bf", [128, 128], BF16)
        self.ident_f32 = dr("ident_f32", [128, 128], F32)
        self.cmask = dr("cmask", [L, 2 * L], BF16)
        self.dqmask = dr("dqmask", [L, 2 * L], BF16)
        self.i_beta = dr("i_beta", [16], U32)
        self.i_bbc = dr("i_bbc", [HPC * self.NSEG], U32)
        self.i_oh = dr("i_oh", [DM], U32)
        self.i_we = dr("i_we", [64], U32)

        self.d_xh_in = di("d_xh_in", [DM, TSL], BF16)
        self.d_xh_all = di("d_xh_all", [NCORE * DM, TSL], BF16)
        self.d_beta_in = di("d_beta_in", [16, TSL])
        self.d_beta_all = di("d_beta_all", [NCORE * 16, TSL])
        self.d_oh_in = di("d_oh_in", [HPC * D, T], BF16)
        self.d_oh_all = di("d_oh_all", [NCORE * HPC * D, T], BF16)
        self.d_x1 = di("d_x1", [TSL, DM])
        self.d_x2T = di("d_x2T", [DM, TSL], BF16)
        self.d_w8T = di("d_w8T", [NE, TSL])

        self.out_sl = nc.dram_tensor("out_sl", [TSL, DM], F32, kind="ExternalOutput").ap()

        self.dbg("xhat", [DM, TSL], BF16)
        self.dbg("beta", [16, TSL])
        self.dbg("qc", [HPC * D, T], BF16)
        self.dbg("kc", [HPC * D, T], BF16)
        self.dbg("vc", [HPC * D, T], BF16)
        self.dbg("ac", [HPC * D, T])
        self.dbg("oraw", [HPC * D, T])
        self.dbg("ohat", [HPC * D, T], BF16)
        self.dbg("x1", [TSL, DM])
        self.dbg("x2T", [DM, TSL], BF16)
        self.dbg("gates", [TSL, NE])
        self.dbg("acc", [TSL, DM])

        ss = self._stack
        for nm in ["s_ld", "s_w", "s_pe", "s_dv", "s_se", "s_st", "s_cc", "s_gp"]:
            setattr(self, nm, Sem(nc, nm, ss))

        with nc.allow_low_precision(reason="bf16 ops blessed by stated tolerance"), nc.Block() as block:
            self.block = block
            self.emit_consts()
            with contextlib.ExitStack() as ph12:
                self.ph12 = ph12
                with contextlib.ExitStack() as ph1:
                    self.ph1 = ph1
                    self.emit_phase1()
                if self.stop_after != "p1":
                    self.emit_gather1()
                    if self.stop_after != "g1":
                        self.emit_phase2()
                        if not self.stop_after.startswith("p2"):
                            self.emit_gather2()
            if self.stop_after in ("full",):
                self.barrier()
                with contextlib.ExitStack() as ph3:
                    self.ph3 = ph3
                    with contextlib.ExitStack() as ph3a:
                        self.ph3a = ph3a
                        self.emit_phase3a()
                    self.barrier()
                    self.emit_phase3b()
            else:
                self.barrier()
        return nc

    def emit_consts(self):
        nc = self.nc
        ld, gp, dv = self.s_ld, self.s_gp, self.s_dv
        self.c_ident_bf = self.sb("c_ident_bf", [128, 128], BF16)
        self.c_ident_f32 = self.sb("c_ident_f32", [128, 128], F32)
        self.c_cmask = self.sb("c_cmask", [L, 2 * L], BF16)
        self.c_dqmask = self.sb("c_dqmask", [L, 2 * L], BF16)
        self.c_lng = self.sb("c_lng", [128, 1], F32)
        self.c_lnb = self.sb("c_lnb", [128, 1], F32)
        self.c_ba = self.sb("c_ba", [128, HPC], F32)
        self.c_cb = self.sb("c_cb", [128, 3 * HPC], F32)
        self.c_wb = self.sb("c_wb", [128, 16 * H], BF16)
        self.c_bb = self.sb("c_bb", [1, H], BF16)
        self.c_gateb = self.sb("c_gateb", [128, NE], F32)
        self.c_ones_bf = self.sb("c_ones_bf", [1, 128], BF16)
        self.c_onesf = self.sb("c_onesf", [128, 128], F32)
        self.c_convw = self.sb("c_convw", [128, 3 * HPC * KC], F32)
        self.c_convd = self.sb("c_convd", [128, 3 * HPC * KC * 128], BF16)

        S = nc.sync
        S.dma_start(out=self.c_ident_bf[:], in_=self.ident_bf).then_inc(ld.h, 16); ld.n += 16
        S.dma_start(out=self.c_ident_f32[:], in_=self.ident_f32).then_inc(ld.h, 16); ld.n += 16
        S.dma_start(out=self.c_cmask[:], in_=self.cmask).then_inc(ld.h, 16); ld.n += 16
        S.dma_start(out=self.c_dqmask[:], in_=self.dqmask).then_inc(ld.h, 16); ld.n += 16
        S.dma_start(out=self.c_lng[:], in_=self.ln_g.unsqueeze(1)).then_inc(ld.h, 16); ld.n += 16
        S.dma_start(out=self.c_lnb[:], in_=self.ln_b.unsqueeze(1)).then_inc(ld.h, 16); ld.n += 16
        S.dma_start(out=self.c_ba[:], in_=self.ba_s.rearrange("(h c) -> c h", h=HPC), allow_slow_non_contiguous=True).then_inc(ld.h, 16); ld.n += 16
        for j, t in enumerate([self.qcb, self.kcb, self.vcb]):
            S.dma_start(out=self.c_cb[:, j * HPC:(j + 1) * HPC],
                        in_=t.rearrange("(h c) -> c h", h=HPC), allow_slow_non_contiguous=True).then_inc(ld.h, 16); ld.n += 16
        for j, t in enumerate([self.qcw, self.kcw, self.vcw]):
            for hh in range(HPC):
                S.dma_start(out=self.c_convw[:, j * HPC * KC + hh * KC: j * HPC * KC + (hh + 1) * KC],
                            in_=t[hh * 128:(hh + 1) * 128, :]).then_inc(ld.h, 16); ld.n += 16
        S.dma_start(out=self.c_gateb[:],
                    in_=self.gate_b.unsqueeze(0).partition_broadcast(128).squeeze(1)).then_inc(ld.h, 16); ld.n += 16
        nc.gpsimd.dma_start(out=self.c_wb[:].rearrange("p (a h) -> p a h", a=16),
                            in_=self.wb_full.rearrange("(a p) h -> p a h", p=128)).then_inc(gp.h, 16); gp.n += 16
        nc.gpsimd.dma_start(out=self.c_bb[:], in_=self.bb_in.unsqueeze(0)).then_inc(gp.h, 16); gp.n += 16
        nc.gpsimd.memset(self.c_ones_bf[:], 1.0)
        gp.inc(nc.gpsimd.memset(self.c_onesf[:], 1.0), 1)
        nc.vector.wait_ge(ld.h, ld.n)
        for j in range(3 * HPC * KC):
            ins = nc.vector.tensor_scalar(self.c_convd[:, j * 128:(j + 1) * 128],
                                          self.c_ident_f32[:], self.c_convw[:, j:j + 1], None, ALU.mult)
        dv.inc(ins)
        self.n_consts_dv = dv.n
        self.n_consts_gp = gp.n

    def emit_ln_tile(self, xti, xc, xsq, stat, xh):
        nc = self.nc
        dv, se = self.s_dv, self.s_se
        n = dv.inc(nc.vector.tensor_reduce(stat[:, 0:1], xti[:], AX.X, ALU.add))
        nc.vector.wait_ge(dv.h, n)
        n = dv.inc(nc.vector.tensor_scalar(stat[:, 1:2], stat[:, 0:1], -1.0 / DM, None, ALU.mult))
        nc.vector.wait_ge(dv.h, n)
        n = dv.inc(nc.vector.tensor_scalar(xc[:], xti[:], stat[:, 1:2], None, ALU.add))
        nc.scalar.wait_ge(dv.h, n)
        n = se.inc(nc.scalar.activation(xsq[:], xc[:], AF.Square))
        nc.vector.wait_ge(se.h, n)
        n = dv.inc(nc.vector.tensor_reduce(stat[:, 2:3], xsq[:], AX.X, ALU.add))
        nc.vector.wait_ge(dv.h, n)
        n = dv.inc(nc.vector.tensor_scalar(stat[:, 2:3], stat[:, 2:3], 1.0 / DM, LN_EPS, ALU.mult, ALU.add))
        nc.scalar.wait_ge(dv.h, n)
        n = se.inc(nc.scalar.activation(stat[:, 2:3], stat[:, 2:3], AF.Sqrt))
        nc.vector.wait_ge(se.h, n)
        n = dv.inc(nc.vector.reciprocal(stat[:, 3:4], stat[:, 2:3]))
        nc.vector.wait_ge(dv.h, n)
        return dv.inc(nc.vector.tensor_scalar(xh[:], xc[:], stat[:, 3:4], None, ALU.mult))

    def emit_phase1(self):
        nc = self.nc
        TSL = self.TSL
        NTT = TSL // 128
        ld, pe, dv, se, st = self.s_ld, self.s_pe, self.s_dv, self.s_se, self.s_st
        sb = lambda n, s_, d_: self.sb(n, s_, d_, self.ph1)

        xti = sb("p1_x", [128, DM], F32)
        xc = sb("p1_xc", [128, DM], F32)
        xsq = sb("p1_xsq", [128, DM], F32)
        stat = sb("p1_stat", [128, 4], F32)
        xh = sb("p1_xh", [128, DM], BF16)
        xhT = sb("p1_xhT", [128, 16 * 128], BF16)
        sbeta = sb("p1_beta", [128, 16], F32)
        sbetaT = sb("p1_betaT", [16, 128], F32)
        self.pmm = self.ps("ps_mm", [128, 512], F32, self.ph12)
        self.pmm2 = self.ps("ps_mm2", [128, 512], F32, self.ph12)
        self.pmisc = self.ps("ps_misc", [128, 512], F32, self.ph12)
        self.pbig = self.pmisc[:, 0:256]
        self.pcdF = self.ps("ps_cd", [64, 256], F32, self.ph12)
        self.pin = self.pmisc[:, 256:384]
        self.ptt = self.ps("ps_tt", [128, 128], BF16, self.ph12)
        self.pttf = self.ps("ps_ttf", [128, 128], F32, self.ph12)
        self.pzh1a = self.ps("ps_zh1a", [128, 512], F32, self.ph12)
        self.pzh1b = self.ps("ps_zh1b", [128, 512], F32, self.ph12)
        ptb = self.ptt
        ptf = self.pttf
        pb8 = self.pmisc[:, 256:272]

        st_prev = 0
        for tt in range(NTT):
            nc.sync.wait_ge(st.h, st_prev)
            nc.sync.wait_ge(dv.h, dv.n)
            nc.sync.dma_start(out=xti[:], in_=self.x_sl[tt * 128:(tt + 1) * 128, :]).then_inc(ld.h, 16); ld.n += 16
            nc.vector.wait_ge(ld.h, ld.n)
            nxh = self.emit_ln_tile(xti, xc, xsq, stat, xh)
            for cb in range(16):
                nc.tensor.wait_ge(dv.h, nxh if cb == 0 else dv.n)
                nmm = pe.inc(nc.tensor.transpose(ptb[:], xh[:, cb * 128:(cb + 1) * 128], self.c_ident_bf[:]))
                nc.vector.wait_ge(pe.h, nmm)
                dv.inc(nc.vector.tensor_copy(xhT[:, cb * 128:(cb + 1) * 128], ptb[:]))
            ntr = dv.n
            nc.tensor.wait_ge(dv.h, ntr)
            nc.tensor.wait_ge(self.s_gp.h, self.n_consts_gp)
            for cb in range(16):
                nc.tensor.matmul(pb8, xhT[:, cb * 128:(cb + 1) * 128],
                                 self.c_wb[:, cb * 16:(cb + 1) * 16], start=(cb == 0), stop=False)
            nmm = pe.inc(nc.tensor.matmul(pb8, self.c_ones_bf[:], self.c_bb[:], start=False, stop=True))
            nc.scalar.wait_ge(pe.h, nmm)
            nsg = se.inc(nc.scalar.activation(sbeta[:], pb8, AF.Sigmoid))
            nc.tensor.wait_ge(se.h, nsg)
            nmm = pe.inc(nc.tensor.transpose(ptf[0:16, :], sbeta[:], self.c_ident_f32[:]))
            nc.vector.wait_ge(pe.h, nmm)
            nev = dv.inc(nc.vector.tensor_copy(sbetaT[:], ptf[0:16, :]))
            nc.scalar.wait_ge(dv.h, nev)
            for cb in range(16):
                st.inc(nc.scalar.dma_start(out=self.d_xh_in[cb * 128:(cb + 1) * 128, tt * 128:(tt + 1) * 128],
                                           in_=xhT[:, cb * 128:(cb + 1) * 128]), 16)
            st.inc(nc.scalar.dma_start(out=self.d_beta_in[:, tt * 128:(tt + 1) * 128], in_=sbetaT[:]), 16)
            st_prev = st.n
        self.p1_store_n = st.n

    def emit_gather1(self):
        nc = self.nc
        cc, st, gp = self.s_cc, self.s_st, self.s_gp
        nc.gpsimd.wait_ge(st.h, self.p1_store_n)
        cc.inc(nc.gpsimd.collective_compute("AllGather", ALU.bypass, [list(range(NCORE))],
                                            [self.d_xh_in.opt()], [self.d_xh_all.opt()]), 1)
        cc.inc(nc.gpsimd.collective_compute("AllGather", ALU.bypass, [list(range(NCORE))],
                                            [self.d_beta_in.opt()], [self.d_beta_all.opt()]), 1)
        self.g1_cc = cc.n
        if "xhat" in self.debug:
            gp.inc(nc.gpsimd.dma_start(out=self.dbgt["xhat"], in_=self.d_xh_in), 16)
        if "beta" in self.debug:
            gp.inc(nc.gpsimd.dma_start(out=self.dbgt["beta"], in_=self.d_beta_in), 16)

    def emit_phase2(self):
        nc = self.nc
        T, TSL, SEG, NSEG, NCS, NBLK = self.T, self.TSL, self.SEG, self.NSEG, self.NCS, self.NBLK
        ld, pe, dv, se, st, gp, w, cc = (self.s_ld, self.s_pe, self.s_dv, self.s_se,
                                         self.s_st, self.s_gp, self.s_w, self.s_cc)
        sb = lambda n, s_, d_: self.sb(n, s_, d_, self.ph12)
        NQ = SEG // 512

        wproj = sb("p2_wproj", [128, 5 * 16 * HPC * 128], BF16)
        wsrc = [self.wq_s, self.wk_s, self.wv_s, self.wa_s, self.wg_s]
        nc.gpsimd.wait_ge(cc.h, self.g1_cc)
        for p in range(5):
            for icb in range(16):
                ins = nc.gpsimd.dma_start(
                    out=wproj[:, ((p * 16 + icb) * HPC) * 128: ((p * 16 + icb) * HPC + HPC) * 128],
                    in_=wsrc[p][icb * 128:(icb + 1) * 128, :])
                w.inc(ins, 16)
        self.p2_w_n = w.n

        def wtile(p, icb, hh):
            off = ((p * 16 + icb) * HPC + hh) * 128
            return wproj[:, off:off + 128]

        beta_own = sb("p2_beta_own", [16, TSL], F32)
        i_beta_t = sb("p2_ibeta", [16, 1], U32)
        betaT = sb("p2_betaT", [128, self.NLB * 16], F32)
        ibbc_t = sb("p2_ibbc", [128, 1], U32)

        nc.sync.dma_start(out=i_beta_t[:], in_=self.i_beta.unsqueeze(1)).then_inc(ld.h, 16); ld.n += 16
        nc.gpsimd.wait_ge(cc.h, self.g1_cc)
        nc.gpsimd.wait_ge(ld.h, ld.n)
        nbo = gp.inc(nc.gpsimd.indirect_dma_start(out=beta_own[:], out_offset=None, in_=self.d_beta_all,
                                                  in_offset=IndirectOffsetOnAxis(ap=i_beta_t[:, 0:1], axis=0)), 16)
        ptf2 = self.pttf
        for lb in range(self.NLB):
            nc.tensor.wait_ge(gp.h, nbo)
            if lb > 0:
                nc.tensor.wait_ge(dv.h, dv.n)
            nmm = pe.inc(nc.tensor.transpose(ptf2[0:128, 0:16], beta_own[:, lb * 128:(lb + 1) * 128],
                                             self.c_ident_f32[0:16, 0:16]))
            nc.vector.wait_ge(pe.h, nmm)
            dv.inc(nc.vector.tensor_copy(betaT[:, lb * 16:(lb + 1) * 16], ptf2[0:128, 0:16]))

        s_cm = sb("p2_scm", [128, HPC * 128], BF16)
        s_km = sb("p2_skm", [128, HPC * 128], BF16)
        nc.vector.memset(s_cm[:], 0.0)
        dv.inc(nc.vector.memset(s_km[:], 0.0))

        xh_seg = sb("p2_xh", [128, 16 * SEG], BF16)
        qpad = sb("p2_qpad", [128, HPC * (SEG + 3)], BF16)
        kpad = sb("p2_kpad", [128, HPC * (SEG + 3)], BF16)
        vpad = sb("p2_vpad", [128, HPC * (SEG + 3)], BF16)
        a_cm = sb("p2_acm", [128, HPC * SEG], F32)
        g_cm = sb("p2_gcm", [128, HPC * SEG], BF16)
        q_cm = sb("p2_qcm", [128, HPC * SEG], BF16)
        k_cm = sb("p2_kcm", [128, HPC * SEG], BF16)
        v_cm = sb("p2_vcm", [128, HPC * SEG], BF16)
        ktb_cm = sb("p2_ktb", [128, HPC * SEG], BF16)
        bbc = sb("p2_bbc", [128, SEG], BF16)
        p_cm = sb("p2_pcm", [128, HPC * SEG], F32)
        ktm = sb("p2_ktm", [L, HPC * NCS * 128], BF16)
        vtm = sb("p2_vtm", [128, NBLK * HPC * 128], BF16)
        ratm = sb("p2_ratm", [128, NBLK * HPC * 128], BF16)
        rtm = sb("p2_rtm", [128, NBLK * HPC * 128], BF16)
        o_cm = sb("p2_ocm", [128, HPC * SEG], F32)
        osq = sb("p2_osq", [128, SEG], F32)
        orow = sb("p2_orow", [128, SEG], F32)
        orow2 = sb("p2_orow2", [128, SEG], F32)
        obc = sb("p2_obc", [128, SEG], F32)
        obc2 = sb("p2_obc2", [128, SEG], F32)
        oh_seg = sb("p2_ohseg", [128, HPC * SEG], BF16)

        bhat = sb("p2_bhat", [L, 2 * 128], BF16)
        t_b1 = sb("p2_tb1", [L, 2 * 128], F32)
        t_b2 = sb("p2_tb2", [L, 2 * 128], F32)
        t1t = sb("p2_t1", [L, 2 * 128], BF16)
        xcur = sb("p2_xcur", [L, 2 * 128], BF16)
        yfin = t_b1
        w_tm = sb("p2_wtm", [L, 2 * 128], BF16)
        ct_sb = sb("p2_ctsb", [L, 2 * L], BF16)
        dq_sb = sb("p2_dqsb", [L, 2 * L], BF16)
        diagp = sb("p2_diagp", [128, 128], BF16)
        inner_cm = sb("p2_inner", [128, 128], BF16)

        pmm, pmm2 = self.pmm, self.pmm2
        pz0 = self.pmm[0:L, 0:256]
        pz1 = self.pmm2[0:L, 0:256]
        pks = self.pbig[0:L, 0:256]
        pcd = self.pcdF[0:L, 0:2 * L]
        pcd2 = self.pcdF[0:L, 128:128 + 2 * L]
        po = self.pbig[:, 0:2 * L]
        pin = self.pin
        pkm = self.pin
        ptt = self.ptt
        pttf = self.pttf
        pst = self.pmm[0:1, 0:512]
        evac_n = {}

        stage = self.stop_after
        if stage == 'p2a':
            self.p2_store_n = st.n
            return
        seg_limit = 1 if (stage.startswith("p2") and stage != "p2") else NSEG
        if stage.startswith('p2s') and len(stage) > 3:
            seg_limit = int(stage[3:])
        for seg in range(seg_limit):
            t0 = seg * SEG
            rblk = t0 // TSL
            tloc = t0 - rblk * TSL
            self.barrier()

            for icb in range(16):
                nc.sync.dma_start(out=xh_seg[:, icb * SEG:(icb + 1) * SEG],
                                  in_=self.d_xh_all[rblk * DM + icb * 128: rblk * DM + (icb + 1) * 128,
                                                    tloc:tloc + SEG]).then_inc(ld.h, 16)
                ld.n += 16
            n_xh = ld.n

            for buf in (qpad, kpad, vpad):
                for hh in range(HPC):
                    base = hh * (SEG + 3)
                    if seg == 0:
                        ins = nc.vector.memset(buf[:, base:base + 3], 0.0)
                    else:
                        ins = nc.vector.tensor_copy(buf[:, base:base + 3], buf[:, base + SEG:base + SEG + 3])
            n_pad = dv.inc(ins)

            for p, dst in [(0, qpad), (1, kpad), (2, vpad), (3, a_cm), (4, g_cm)]:
                for hh in range(HPC):
                    for th in range(NQ):
                        pp = pmm if (th % 2 == 0) else pmm2
                        nc.tensor.wait_ge(ld.h, n_xh)
                        nc.tensor.wait_ge(self.s_w.h, self.p2_w_n)
                        nc.tensor.wait_ge(se.h, evac_n.get(id(pp), 0))
                        for icb in range(16):
                            ins = nc.tensor.matmul(pp[:], wtile(p, icb, hh),
                                                   xh_seg[:, icb * SEG + th * 512: icb * SEG + th * 512 + 512],
                                                   start=(icb == 0), stop=(icb == 15))
                        nmm = pe.inc(ins)
                        nc.scalar.wait_ge(pe.h, nmm)
                        if p <= 2:
                            out = dst[:, hh * (SEG + 3) + 3 + th * 512: hh * (SEG + 3) + 3 + th * 512 + 512]
                            ins = nc.scalar.copy(out, pp[:])
                        elif p == 3:
                            out = dst[:, hh * SEG + th * 512: hh * SEG + th * 512 + 512]
                            ins = nc.scalar.activation(out, pp[:], AF.Sigmoid, bias=self.c_ba[:, hh:hh + 1])
                        else:
                            out = dst[:, hh * SEG + th * 512: hh * SEG + th * 512 + 512]
                            ins = nc.scalar.activation(out, pp[:], AF.Sigmoid)
                        evac_n[id(pp)] = se.inc(ins)
            n_proj = se.n
            if stage == 'p2b':
                continue

            for p, (src, dst) in enumerate([(qpad, q_cm), (kpad, k_cm), (vpad, v_cm)]):
                for hh in range(HPC):
                    for th in range(NQ):
                        pp = pmm if (th % 2 == 0) else pmm2
                        nc.tensor.wait_ge(se.h, n_proj)
                        nc.tensor.wait_ge(dv.h, max(n_pad, self.n_consts_dv))
                        nc.tensor.wait_ge(se.h, evac_n.get(id(pp), 0))
                        for j in range(KC):
                            coff = ((p * HPC + hh) * KC + j) * 128
                            moff = hh * (SEG + 3) + th * 512 + j
                            ins = nc.tensor.matmul(pp[:], self.c_convd[:, coff:coff + 128],
                                                   src[:, moff:moff + 512],
                                                   start=(j == 0), stop=(j == KC - 1))
                        nmm = pe.inc(ins)
                        nc.scalar.wait_ge(pe.h, nmm)
                        out = dst[:, hh * SEG + th * 512: hh * SEG + th * 512 + 512]
                        ins = nc.scalar.activation(out, pp[:], AF.Silu,
                                                   bias=self.c_cb[:, p * HPC + hh:p * HPC + hh + 1])
                        evac_n[id(pp)] = se.inc(ins)
            n_conv = se.n
            if stage == 'p2c':
                continue

            for nm, srcb in [("qc", q_cm), ("kc", k_cm), ("vc", v_cm), ("ac", a_cm)]:
                if nm in self.dbgt:
                    nc.scalar.wait_ge(se.h, n_conv)
                    for hh in range(HPC):
                        st.inc(nc.scalar.dma_start(
                            out=self.dbgt[nm][hh * 128:(hh + 1) * 128, t0:t0 + SEG],
                            in_=srcb[:, hh * SEG:(hh + 1) * SEG]), 16)

            for hh in range(HPC):
                nc.sync.wait_ge(dv.h, dv.n)
                nc.sync.dma_start(out=ibbc_t[:],
                                  in_=self.i_bbc[hh * NSEG + seg: hh * NSEG + seg + 1]
                                  .unsqueeze(0).partition_broadcast(128).squeeze(1)).then_inc(ld.h, 16)
                ld.n += 16
                nc.gpsimd.wait_ge(ld.h, ld.n)
                ngb = gp.inc(nc.gpsimd.indirect_dma_start(
                    out=bbc[:], out_offset=None,
                    in_=self.d_beta_all.rearrange("r (s t) -> (r s) t", s=TSL // SEG),
                    in_offset=IndirectOffsetOnAxis(ap=ibbc_t[:, 0:1], axis=0)), 16)
                nc.vector.wait_ge(gp.h, ngb)
                nc.vector.wait_ge(se.h, n_conv)
                dv.inc(nc.vector.tensor_tensor(ktb_cm[:, hh * SEG:(hh + 1) * SEG],
                                               k_cm[:, hh * SEG:(hh + 1) * SEG], bbc[:], ALU.mult))
            n_ktb = dv.n

            nc.vector.wait_ge(se.h, n_proj)
            nc.vector.wait_ge(pe.h, pe.n)
            for hh in range(HPC):
                for j in range(NCS):
                    c0 = hh * SEG + j * L
                    ins = nc.vector.tensor_tensor_scan(p_cm[:, c0:c0 + L], a_cm[:, c0:c0 + L],
                                                       a_cm[:, c0:c0 + L], 1.0, ALU.mult, ALU.bypass)
            n_pscan = dv.inc(ins)

            for blk in range(NBLK):
                for hh in range(HPC):
                    cbase = hh * SEG + blk * 128
                    fr = (blk * HPC + hh) * 128
                    nc.tensor.wait_ge(se.h, n_conv)
                    nc.tensor.wait_ge(dv.h, dv.n)
                    nmm = pe.inc(nc.tensor.transpose(ptt[:], k_cm[:, cbase:cbase + 128], self.c_ident_bf[:]))
                    nc.vector.wait_ge(pe.h, nmm)
                    j0 = blk * (128 // L)
                    for cc in range(128 // L):
                        ins = nc.vector.tensor_copy(
                            ktm[:, (hh * NCS + j0 + cc) * 128:(hh * NCS + j0 + cc) * 128 + 128],
                            ptt[cc * L:(cc + 1) * L, :])
                    dv.inc(ins)
                    nc.tensor.wait_ge(dv.h, dv.n)
                    nmm = pe.inc(nc.tensor.transpose(ptt[:], v_cm[:, cbase:cbase + 128], self.c_ident_bf[:]))
                    nc.vector.wait_ge(pe.h, nmm)
                    dv.inc(nc.vector.tensor_copy(vtm[:, fr:fr + 128], ptt[:]))
                    nc.tensor.wait_ge(dv.h, dv.n)
                    nmm = pe.inc(nc.tensor.transpose(pttf[:], a_cm[:, cbase:cbase + 128], self.c_ident_f32[:]))
                    nc.vector.wait_ge(pe.h, nmm)
                    dv.inc(nc.vector.reciprocal(ratm[:, fr:fr + 128], pttf[:]))
                    nc.tensor.wait_ge(dv.h, max(dv.n, n_pscan))
                    nmm = pe.inc(nc.tensor.transpose(pttf[:], p_cm[:, cbase:cbase + 128], self.c_ident_f32[:]))
                    nc.vector.wait_ge(pe.h, nmm)
                    dv.inc(nc.vector.reciprocal(rtm[:, fr:fr + 128], pttf[:]))
            n_tm = dv.n
            if stage == 'p2d':
                continue

            if stage.startswith('p2c') and len(stage) > 3:
                ncs_limit = int(stage[3:])
            else:
                ncs_limit = 1 if stage.startswith('p2e') else NCS
            for j in range(ncs_limit):
                tglob = t0 + j * L
                gb = tglob // 128
                lb = gb % self.NLB
                rr = tglob // TSL
                NPB = 128 // L
                poff = L * (j % NPB)
                blk = j // NPB
                tcol = lambda hh: hh * SEG + j * L
                frs = slice((blk * HPC) * 128, (blk * HPC + 2) * 128)

                nc.tensor.wait_ge(dv.h, dv.n)
                nc.tensor.wait_ge(se.h, n_conv)
                for hh in range(HPC):
                    ins = nc.tensor.matmul(pks[:, hh * 128:(hh + 1) * 128],
                                           k_cm[:, tcol(hh):tcol(hh) + L],
                                           s_km[:, hh * 128:(hh + 1) * 128],
                                           start=True, stop=True)
                n_ks = pe.inc(ins)

                for hh in range(HPC):
                    nc.tensor.matmul(pcd[:, hh * L:(hh + 1) * L],
                                     ktb_cm[:, tcol(hh):tcol(hh) + L],
                                     k_cm[:, tcol(hh):tcol(hh) + L],
                                     start=True, stop=True)
                for hh in range(HPC):
                    ins = nc.tensor.matmul(pcd2[:, hh * L:(hh + 1) * L],
                                           k_cm[:, tcol(hh):tcol(hh) + L],
                                           q_cm[:, tcol(hh):tcol(hh) + L],
                                           start=True, stop=True)
                n_cd = pe.inc(ins)

                ra_sl = ratm[poff:poff + L, frs]
                rt_sl = rtm[poff:poff + L, frs]
                vt_sl = vtm[poff:poff + L, frs]
                nc.vector.wait_ge(pe.h, n_ks)
                nc.vector.tensor_tensor(t_b2[:], rt_sl, vt_sl, ALU.mult)
                n1 = dv.inc(nc.vector.scalar_tensor_tensor(t_b1[:], pks[:], SCALE, ra_sl, ALU.mult, ALU.mult))
                nc.vector.wait_ge(dv.h, n1)
                nbh = dv.inc(nc.vector.tensor_tensor(bhat[:], t_b1[:], t_b2[:], ALU.subtract))

                nc.vector.wait_ge(pe.h, n_cd)
                nc.vector.tensor_tensor(ct_sb[:], pcd[:], self.c_cmask[:], ALU.mult)
                nmask = dv.inc(nc.vector.tensor_tensor(dq_sb[:], pcd2[:], self.c_dqmask[:], ALU.mult))
                if stage == 'p2e1':
                    continue

                zb = [[pz0[:, 0:128], pz1[:, 0:128]],
                      [self.pzh1a[0:L, 0:128], self.pzh1b[0:L, 0:128]]]
                last_t1 = [max(nbh, nmask), max(nbh, nmask)]
                last_add = [gp.n, gp.n]
                last_mm = [0, 0]
                for m in range(1, L):
                    for hh in range(HPC):
                        pzh = zb[hh][m % 2]
                        hsl = slice(hh * 128, (hh + 1) * 128)
                        if m == 1:
                            nc.tensor.wait_ge(dv.h, last_t1[hh])
                            ins = nc.tensor.matmul(pzh, ct_sb[:, hh * L:(hh + 1) * L],
                                                   bhat[:, hsl], start=True, stop=True)
                        else:
                            nc.tensor.wait_ge(gp.h, last_add[hh])
                            ins = nc.tensor.matmul(pzh, ct_sb[:, hh * L:(hh + 1) * L],
                                                   xcur[:, hsl], start=True, stop=True)
                        last_mm[hh] = pe.inc(ins)
                    for hh in range(HPC):
                        pzh = zb[hh][m % 2]
                        hsl = slice(hh * 128, (hh + 1) * 128)
                        nc.vector.wait_ge(pe.h, last_mm[hh])
                        last_t1[hh] = dv.inc(nc.vector.scalar_tensor_tensor(
                            t1t[:, hsl], pzh, -1.0,
                            ratm[poff:poff + L, (blk * HPC + hh) * 128:(blk * HPC + hh) * 128 + 128],
                            ALU.mult, ALU.mult))
                        nc.gpsimd.wait_ge(dv.h, last_t1[hh])
                        last_add[hh] = gp.inc(nc.gpsimd.tensor_tensor(
                            xcur[:, hsl], bhat[:, hsl], t1t[:, hsl], ALU.add), 1)

                if stage == 'p2e2':
                    continue
                nc.vector.wait_ge(gp.h, max(last_add))
                for hh in range(HPC):
                    bcol = betaT[poff:poff + L, lb * 16 + hh * 8 + rr: lb * 16 + hh * 8 + rr + 1]
                    ins = nc.vector.tensor_scalar(w_tm[:, hh * 128:(hh + 1) * 128],
                                                  xcur[:, hh * 128:(hh + 1) * 128],
                                                  bcol, -SCALE, ALU.mult, ALU.mult)
                nwtm = dv.inc(ins)

                if stage == 'p2e3':
                    continue
                nc.tensor.wait_ge(dv.h, nwtm)
                for hh in range(HPC):
                    nc.tensor.matmul(po[:, hh * L:(hh + 1) * L],
                                     s_km[:, hh * 128:(hh + 1) * 128],
                                     q_cm[:, tcol(hh):tcol(hh) + L],
                                     start=True, stop=False)
                    ins = nc.tensor.matmul(po[:, hh * L:(hh + 1) * L],
                                           w_tm[:, hh * 128:(hh + 1) * 128],
                                           dq_sb[:, hh * L:(hh + 1) * L],
                                           start=False, stop=True)
                n_po = pe.inc(ins)
                nc.vector.wait_ge(pe.h, n_po)
                for hh in range(HPC):
                    ins = nc.vector.tensor_tensor(o_cm[:, tcol(hh):tcol(hh) + L],
                                                  p_cm[:, tcol(hh):tcol(hh) + L],
                                                  po[:, hh * L:(hh + 1) * L], ALU.mult)
                dv.inc(ins)

                if stage == 'p2e4':
                    continue
                for hh in range(HPC):
                    nc.tensor.wait_ge(dv.h, dv.n)
                    nc.tensor.matmul(pin[:], self.c_ident_bf[:], s_cm[:, hh * 128:(hh + 1) * 128],
                                     start=True, stop=False)
                    ins = nc.tensor.matmul(pin[:], w_tm[:, hh * 128:(hh + 1) * 128],
                                           ktm[:, (hh * NCS + j) * 128:(hh * NCS + j) * 128 + 128],
                                           start=False, stop=True)
                    nin = pe.inc(ins)
                    nc.vector.wait_ge(pe.h, nin)
                    nev = dv.inc(nc.vector.tensor_copy(inner_cm[:], pin[:]))
                    pcol = p_cm[:, tcol(hh) + L - 1: tcol(hh) + L]
                    nc.vector.wait_ge(dv.h, nev)
                    nc.vector.tensor_scalar(s_cm[:, hh * 128:(hh + 1) * 128], inner_cm[:], pcol, None, ALU.mult)
                    ndg = dv.inc(nc.vector.tensor_scalar(diagp[:], self.c_ident_bf[:], pcol, None, ALU.mult))
                    nc.tensor.wait_ge(dv.h, ndg)
                    nkm = pe.inc(nc.tensor.matmul(pkm[:], inner_cm[:], diagp[:], start=True, stop=True))
                    nc.vector.wait_ge(pe.h, nkm)
                    dv.inc(nc.vector.tensor_copy(s_km[:, hh * 128:(hh + 1) * 128], pkm[:]))

            n_scan = dv.n
            if stage.startswith('p2c') or stage.startswith('p2e'):
                continue
            if "oraw" in self.dbgt:
                nc.scalar.wait_ge(dv.h, n_scan)
                for hh in range(HPC):
                    st.inc(nc.scalar.dma_start(
                        out=self.dbgt["oraw"][hh * 128:(hh + 1) * 128, t0:t0 + SEG],
                        in_=o_cm[:, hh * SEG:(hh + 1) * SEG]), 16)
            for hh in range(HPC):
                nc.scalar.wait_ge(dv.h, n_scan)
                nc.scalar.wait_ge(dv.h, dv.n)
                nsq = se.inc(nc.scalar.activation(osq[:], o_cm[:, hh * SEG:(hh + 1) * SEG], AF.Square))
                for q in range(NQ):
                    nc.tensor.wait_ge(dv.h, dv.n)
                    nmm = pe.inc(nc.tensor.matmul(pst, self.c_onesf[:, 0:1],
                                                  o_cm[:, hh * SEG + q * 512: hh * SEG + q * 512 + 512],
                                                  start=True, stop=True))
                    nc.vector.wait_ge(pe.h, nmm)
                    nev = dv.inc(nc.vector.tensor_copy(orow[0:1, q * 512: q * 512 + 512], pst))
                    nc.tensor.wait_ge(se.h, nsq)
                    nc.tensor.wait_ge(dv.h, nev)
                    nmm = pe.inc(nc.tensor.matmul(pst, self.c_onesf[:, 0:1],
                                                  osq[:, q * 512: q * 512 + 512], start=True, stop=True))
                    nc.vector.wait_ge(pe.h, nmm)
                    dv.inc(nc.vector.tensor_copy(orow2[0:1, q * 512: q * 512 + 512], pst))
                nc.vector.wait_ge(dv.h, dv.n)
                n0 = dv.inc(nc.vector.tensor_scalar(orow[0:1, :], orow[0:1, :], 1.0 / D, None, ALU.mult))
                nc.vector.wait_ge(dv.h, n0)
                n1 = dv.inc(nc.vector.tensor_scalar(orow2[0:1, :], orow2[0:1, :], 1.0 / D, None, ALU.mult))
                nc.vector.wait_ge(dv.h, n1)
                n2 = dv.inc(nc.vector.tensor_tensor(osq[0:1, :], orow[0:1, :], orow[0:1, :], ALU.mult))
                nc.vector.wait_ge(dv.h, n2)
                n3 = dv.inc(nc.vector.tensor_tensor(orow2[0:1, :], orow2[0:1, :], osq[0:1, :], ALU.subtract))
                nc.vector.wait_ge(dv.h, n3)
                n3b = dv.inc(nc.vector.tensor_scalar(orow2[0:1, :], orow2[0:1, :], LN_EPS, None, ALU.add))
                nc.scalar.wait_ge(dv.h, n3b)
                n4 = se.inc(nc.scalar.activation(orow2[0:1, :], orow2[0:1, :], AF.Sqrt))
                nc.vector.wait_ge(se.h, n4)
                n5 = dv.inc(nc.vector.reciprocal(orow2[0:1, :], orow2[0:1, :]))
                for q in range(NQ):
                    nc.tensor.wait_ge(dv.h, max(n5, dv.n))
                    nc.tensor.wait_ge(se.h, se.n)
                    nc.tensor.matmul(pmm[:], self.c_onesf[0:1, :],
                                     orow[0:1, q * 512: q * 512 + 512],
                                     start=True, stop=True)
                    nmm = pe.inc(nc.tensor.matmul(pmm2[:], self.c_onesf[0:1, :],
                                                  orow2[0:1, q * 512: q * 512 + 512],
                                                  start=True, stop=True))
                    nc.vector.wait_ge(pe.h, nmm)
                    nc.vector.tensor_copy(obc[:, q * 512:q * 512 + 512], pmm[:])
                    dv.inc(nc.vector.tensor_copy(obc2[:, q * 512:q * 512 + 512], pmm2[:]))
                nc.vector.wait_ge(dv.h, dv.n)
                nc.vector.wait_ge(se.h, nsq)
                nc.vector.tensor_tensor(osq[:], o_cm[:, hh * SEG:(hh + 1) * SEG], obc[:], ALU.subtract)
                nc.vector.tensor_tensor(osq[:], osq[:], obc2[:], ALU.mult)
                nc.vector.tensor_scalar(osq[:], osq[:], self.c_lng[:, 0:1], self.c_lnb[:, 0:1], ALU.mult, ALU.add)
                nst = dv.inc(nc.vector.tensor_tensor(oh_seg[:, hh * SEG:(hh + 1) * SEG], osq[:],
                                                     g_cm[:, hh * SEG:(hh + 1) * SEG], ALU.mult))
                nc.scalar.wait_ge(dv.h, nst)
                st.inc(nc.scalar.dma_start(out=self.d_oh_in[hh * 128:(hh + 1) * 128, t0:t0 + SEG],
                                           in_=oh_seg[:, hh * SEG:(hh + 1) * SEG]), 16)
        self.p2_store_n = st.n

    def emit_gather2(self):
        nc = self.nc
        cc, st, gp = self.s_cc, self.s_st, self.s_gp
        nc.gpsimd.wait_ge(st.h, self.p2_store_n)
        cc.inc(nc.gpsimd.collective_compute("AllGather", ALU.bypass, [list(range(NCORE))],
                                            [self.d_oh_in.opt()], [self.d_oh_all.opt()]), 1)
        self.g2_cc = cc.n
        if "ohat" in self.dbgt:
            gp.inc(nc.gpsimd.dma_start(out=self.dbgt["ohat"], in_=self.d_oh_in), 16)

    def emit_phase3a(self):
        nc = self.nc
        T, TSL = self.T, self.TSL
        NTT = TSL // 128
        ld, pe, dv, se, st, gp, w, cc = (self.s_ld, self.s_pe, self.s_dv, self.s_se,
                                         self.s_st, self.s_gp, self.s_w, self.s_cc)
        sb = lambda n, s_, d_: self.sb(n, s_, d_, self.ph3a)

        oh_sl = sb("p3_oh", [128, 16 * TSL], BF16)
        i_oh_t = sb("p3_ioh", [128, 16], U32)
        nc.sync.dma_start(out=i_oh_t[:], in_=self.i_oh.rearrange("(p a) -> p a", p=128)).then_inc(ld.h, 16); ld.n += 16
        nc.gpsimd.wait_ge(cc.h, self.g2_cc)
        nc.gpsimd.wait_ge(ld.h, ld.n)
        for cb in range(16):
            ins = nc.gpsimd.indirect_dma_start(
                out=oh_sl[:, cb * TSL:(cb + 1) * TSL], out_offset=None,
                in_=self.d_oh_all.rearrange("c (b t) -> (c b) t", b=NCORE),
                in_offset=IndirectOffsetOnAxis(ap=i_oh_t[:, cb:cb + 1], axis=0))
            gp.inc(ins, 16)
        n_oh = gp.n

        wo_buf = sb("p3_wob", [128, 16 * 512], BF16)
        xres = sb("p3_xres", [128, 512], F32)
        x1t = sb("p3_x1t", [128, 512], F32)
        self.pA = self.ps("p3_pA", [128, 512], F32, self.ph3)
        self.pB = self.ps("p3_pB", [128, 512], F32, self.ph3)
        pA, pB = self.pA, self.pB
        pmm = pA
        pmm2 = pB
        evac_n = {}
        for ocq in range(4):
            nc.gpsimd.wait_ge(pe.h, pe.n)
            for icb in range(16):
                ins = nc.gpsimd.dma_start(out=wo_buf[:, icb * 512:(icb + 1) * 512],
                                          in_=self.wo[icb * 128:(icb + 1) * 128, ocq * 512:(ocq + 1) * 512])
                w.inc(ins, 16)
            n_wo = w.n
            for tt in range(NTT):
                pp = pmm if (tt % 2 == 0) else pmm2
                nc.tensor.wait_ge(w.h, n_wo)
                nc.tensor.wait_ge(gp.h, n_oh)
                nc.tensor.wait_ge(dv.h, evac_n.get(id(pp), 0))
                for icb in range(16):
                    ins = nc.tensor.matmul(pp[:], oh_sl[:, icb * TSL + tt * 128: icb * TSL + tt * 128 + 128],
                                           wo_buf[:, icb * 512:(icb + 1) * 512],
                                           start=(icb == 0), stop=(icb == 15))
                nmm = pe.inc(ins)
                nc.sync.wait_ge(dv.h, dv.n)
                nc.sync.dma_start(out=xres[:], in_=self.x_sl[tt * 128:(tt + 1) * 128,
                                                             ocq * 512:(ocq + 1) * 512]).then_inc(ld.h, 16)
                ld.n += 16
                nc.vector.wait_ge(pe.h, nmm)
                nc.vector.wait_ge(ld.h, ld.n)
                nev = dv.inc(nc.vector.tensor_tensor(x1t[:], pp[:], xres[:], ALU.add))
                evac_n[id(pp)] = nev
                nc.scalar.wait_ge(dv.h, nev)
                st.inc(nc.scalar.dma_start(out=self.d_x1[tt * 128:(tt + 1) * 128, ocq * 512:(ocq + 1) * 512],
                                           in_=x1t[:]), 16)
        n_x1 = st.n
        if "x1" in self.dbgt:
            nc.gpsimd.wait_ge(st.h, n_x1)
            gp.inc(nc.gpsimd.dma_start(out=self.dbgt["x1"], in_=self.d_x1), 16)
        self.barrier()

        xti = sb("p3_x", [128, DM], F32)
        xc = sb("p3_xc", [128, DM], F32)
        xsq = sb("p3_xsq", [128, DM], F32)
        stat = sb("p3_stat", [128, 4], F32)
        xh = sb("p3_xh", [128, DM], BF16)
        xhT = sb("p3_xhT", [128, 16 * 128], BF16)
        w8all = sb("p3_w8", [128, NTT * 8], F32)
        s8 = sb("p3_s8", [128, 8], F32)
        sb8 = sb("p3_sb8", [128, 8], F32)
        m8 = sb("p3_m8", [128, 8], F32)
        gwb = sb("p3_gwb", [128, 16 * 8], BF16)
        w8T = sb("p3_w8T", [8, TSL], F32)
        ptb = self.ps("p3_ptb", [128, 128], BF16, self.ph3a)
        ptf = self.ps("p3_ptf", [128, 128], F32, self.ph3a)
        pg8 = self.ps("p3_pg8", [128, 8], F32, self.ph3a)

        gp.inc(nc.gpsimd.dma_start(out=gwb[:].rearrange("p (a e) -> p a e", a=16), in_=self.gate_wT.rearrange("(a p) e -> p a e", p=128)), 16)
        n_gw = gp.n

        for tt in range(NTT):
            nc.sync.wait_ge(st.h, st.n)
            nc.sync.wait_ge(dv.h, dv.n)
            nc.sync.dma_start(out=xti[:], in_=self.d_x1[tt * 128:(tt + 1) * 128, :]).then_inc(ld.h, 16); ld.n += 16
            nc.vector.wait_ge(ld.h, ld.n)
            nxh = self.emit_ln_tile(xti, xc, xsq, stat, xh)
            for cb in range(16):
                nc.tensor.wait_ge(dv.h, nxh if cb == 0 else dv.n)
                nmm = pe.inc(nc.tensor.transpose(ptb[:], xh[:, cb * 128:(cb + 1) * 128], self.c_ident_bf[:]))
                nc.vector.wait_ge(pe.h, nmm)
                dv.inc(nc.vector.tensor_copy(xhT[:, cb * 128:(cb + 1) * 128], ptb[:]))
            ntr = dv.n
            nc.scalar.wait_ge(dv.h, ntr)
            for cb in range(16):
                st.inc(nc.scalar.dma_start(out=self.d_x2T[cb * 128:(cb + 1) * 128, tt * 128:(tt + 1) * 128],
                                           in_=xhT[:, cb * 128:(cb + 1) * 128]), 16)
            nc.tensor.wait_ge(dv.h, ntr)
            nc.tensor.wait_ge(gp.h, n_gw)
            nc.tensor.wait_ge(se.h, se.n)
            for cb in range(16):
                ins = nc.tensor.matmul(pg8[:], xhT[:, cb * 128:(cb + 1) * 128],
                                       gwb[:, cb * 8:(cb + 1) * 8], start=(cb == 0), stop=(cb == 15))
            ngm = pe.inc(ins)
            nc.scalar.wait_ge(pe.h, ngm)
            nsg = se.inc(nc.scalar.activation(s8[:], pg8[:], AF.Sigmoid))
            nc.vector.wait_ge(se.h, nsg)
            n = dv.inc(nc.vector.tensor_tensor(sb8[:], s8[:], self.c_gateb[:], ALU.add))
            nc.vector.wait_ge(dv.h, n)
            n = dv.inc(nc.vector.tensor_reduce(stat[:, 0:1], sb8[:], AX.X, ALU.max))
            nc.vector.wait_ge(dv.h, n)
            n = dv.inc(nc.vector.tensor_scalar(m8[:], sb8[:], stat[:, 0:1], None, ALU.is_ge))
            nc.vector.wait_ge(dv.h, n)
            n = dv.inc(nc.vector.scalar_tensor_tensor(sb8[:], m8[:], -BIGNEG, sb8[:], ALU.mult, ALU.add))
            nc.vector.wait_ge(dv.h, n)
            n = dv.inc(nc.vector.tensor_reduce(stat[:, 1:2], sb8[:], AX.X, ALU.max))
            nc.vector.wait_ge(dv.h, n)
            n = dv.inc(nc.vector.tensor_scalar(sb8[:], sb8[:], stat[:, 1:2], None, ALU.is_ge))
            nc.vector.wait_ge(dv.h, n)
            n = dv.inc(nc.vector.tensor_tensor(m8[:], m8[:], sb8[:], ALU.add))
            nc.vector.wait_ge(dv.h, n)
            n = dv.inc(nc.vector.tensor_tensor(m8[:], m8[:], s8[:], ALU.mult))
            nc.vector.wait_ge(dv.h, n)
            n = dv.inc(nc.vector.tensor_reduce(stat[:, 2:3], m8[:], AX.X, ALU.add))
            nc.vector.wait_ge(dv.h, n)
            n = dv.inc(nc.vector.reciprocal(stat[:, 3:4], stat[:, 2:3]))
            nc.vector.wait_ge(dv.h, n)
            nw8 = dv.inc(nc.vector.tensor_scalar(w8all[:, tt * 8:(tt + 1) * 8], m8[:], stat[:, 3:4], None, ALU.mult))
            nc.tensor.wait_ge(dv.h, nw8)
            nmm = pe.inc(nc.tensor.transpose(ptf[0:8, :], w8all[:, tt * 8:(tt + 1) * 8], self.c_ident_f32[:]))
            nc.vector.wait_ge(pe.h, nmm)
            dv.inc(nc.vector.tensor_copy(w8T[:, tt * 128:(tt + 1) * 128], ptf[0:8, :]))
        n_gates = dv.n
        nc.scalar.wait_ge(dv.h, n_gates)
        st.inc(nc.scalar.dma_start(out=self.d_w8T, in_=w8T[:]), 16)
        if "gates" in self.dbgt:
            nc.scalar.wait_ge(dv.h, n_gates)
            for tt in range(NTT):
                st.inc(nc.scalar.dma_start(out=self.dbgt["gates"][tt * 128:(tt + 1) * 128, :],
                                           in_=w8all[:, tt * 8:(tt + 1) * 8]), 16)
        if "x2T" in self.dbgt:
            nc.gpsimd.wait_ge(st.h, st.n)
            gp.inc(nc.gpsimd.dma_start(out=self.dbgt["x2T"], in_=self.d_x2T), 16)

    def emit_phase3b(self):
        nc = self.nc
        T, TSL = self.T, self.TSL
        NTT = TSL // 128
        ld, pe, dv, se, st, gp, w, cc = (self.s_ld, self.s_pe, self.s_dv, self.s_se,
                                         self.s_st, self.s_gp, self.s_w, self.s_cc)
        sb = lambda n, s_, d_: self.sb(n, s_, d_, self.ph3)
        pA, pB = self.pA, self.pB
        evac_n = {}

        NH = max(1, TSL // 1024)
        HALF = TSL // NH
        NTTH = HALF // 128
        NQH = HALF // 512
        x2h = sb("p3_x2h", [128, 16 * HALF], BF16)
        harena = sb("p3_h", [128, 22 * HALF], BF16)
        sgt = sb("p3_sgt", [128, HALF], BF16)
        upbuf0 = sb("p3_up0", [128, 16 * 128], BF16)
        upbuf1 = sb("p3_up1", [128, 16 * 128], BF16)
        dwbuf = sb("p3_dw", [128, 22 * 512], BF16)
        webc = sb("p3_webc", [128, HALF], F32)
        iwe_t = sb("p3_iwe", [128, 1], U32)
        acc = sb("p3_acc", [128, NTTH * DM], F32)
        xf1 = sb("p3_xf1", [128, 512], F32)
        pup = pA
        pup2 = pB
        pdn = pA
        pdn2 = pB
        ub_last = {0: 0, 1: 0}

        for half in range(NH):
            h0 = half * HALF
            self.barrier()
            for icb in range(16):
                nc.sync.dma_start(out=x2h[:, icb * HALF:(icb + 1) * HALF],
                                  in_=self.d_x2T[icb * 128:(icb + 1) * 128, h0:h0 + HALF]).then_inc(ld.h, 16)
                ld.n += 16
            n_x2 = ld.n

            experts = ["sh"] + list(range(NE))
            prev_n_h = 0
            for e in experts:
                n_up_oc = 44 if e == "sh" else 22
                n_hs = n_up_oc // 2
                upw = self.shared_gwT if e == "sh" else self.up_wT[e]
                if e != "sh":
                    nc.sync.wait_ge(st.h, st.n)
                    nc.sync.wait_ge(dv.h, dv.n)
                    nc.sync.dma_start(out=iwe_t[:],
                                      in_=self.i_we[e * NH + half: e * NH + half + 1]
                                      .unsqueeze(0).partition_broadcast(128).squeeze(1)).then_inc(ld.h, 16)
                    ld.n += 16
                    nc.gpsimd.wait_ge(ld.h, ld.n)
                    nwe = gp.inc(nc.gpsimd.indirect_dma_start(
                        out=webc[:], out_offset=None,
                        in_=self.d_w8T.rearrange("e (nh t) -> (e nh) t", nh=NH),
                        in_offset=IndirectOffsetOnAxis(ap=iwe_t[:, 0:1], axis=0)), 16)
                n_webc = gp.n

                nw_pair = [0, 0]
                for pair in range(n_hs):
                    for side, ocb in enumerate([pair, pair + n_hs]):
                        ub = upbuf0 if side == 0 else upbuf1
                        nc.gpsimd.wait_ge(pe.h, ub_last[side])
                        ins = nc.gpsimd.dma_start(
                            out=ub[:].rearrange("p (a o) -> p a o", a=16),
                            in_=upw[:, ocb * 128:(ocb + 1) * 128].rearrange("(a p) o -> p a o", p=128))
                        nw_pair[side] = w.inc(ins, 16)
                    for qq in range(NQH):
                        flip = (pair * NQH + qq) % 2 == 0
                        ppg = pA if flip else pB
                        ppu = pB if flip else pA
                        hsl = slice(pair * HALF + qq * 512, pair * HALF + qq * 512 + 512)
                        ssl = slice(qq * 512, qq * 512 + 512)
                        nc.tensor.wait_ge(w.h, nw_pair[0])
                        nc.tensor.wait_ge(ld.h, n_x2)
                        nc.tensor.wait_ge(dv.h, evac_n.get(id(ppg), 0))
                        nc.tensor.wait_ge(se.h, evac_n.get("se" + str(id(ppg)), 0))
                        for icb in range(16):
                            ins = nc.tensor.matmul(ppg[:], upbuf0[:, icb * 128:(icb + 1) * 128],
                                                   x2h[:, icb * HALF + qq * 512: icb * HALF + qq * 512 + 512],
                                                   start=(icb == 0), stop=(icb == 15))
                        nmmg = pe.inc(ins)
                        nc.tensor.wait_ge(w.h, nw_pair[1])
                        nc.tensor.wait_ge(dv.h, evac_n.get(id(ppu), 0))
                        nc.tensor.wait_ge(se.h, evac_n.get("se" + str(id(ppu)), 0))
                        for icb in range(16):
                            ins = nc.tensor.matmul(ppu[:], upbuf1[:, icb * 128:(icb + 1) * 128],
                                                   x2h[:, icb * HALF + qq * 512: icb * HALF + qq * 512 + 512],
                                                   start=(icb == 0), stop=(icb == 15))
                        nmmu = pe.inc(ins)
                        if e != "sh":
                            nc.scalar.wait_ge(pe.h, nmmg)
                            nc.scalar.wait_ge(dv.h, dv.n)
                            nse = se.inc(nc.scalar.activation(sgt[:, ssl], ppg[:], AF.Silu))
                            evac_n["se" + str(id(ppg))] = nse
                            nc.vector.wait_ge(se.h, nse)
                            nc.vector.wait_ge(pe.h, nmmu)
                            nc.vector.wait_ge(gp.h, n_webc)
                            nev = dv.inc(nc.vector.tensor_tensor(harena[:, hsl], sgt[:, ssl], ppu[:], ALU.mult))
                            nc.vector.wait_ge(dv.h, nev)
                            evac_n[id(ppu)] = dv.inc(nc.vector.tensor_tensor(
                                harena[:, hsl], harena[:, hsl], webc[:, ssl], ALU.mult))
                        else:
                            nc.scalar.wait_ge(pe.h, nmmg)
                            nc.scalar.wait_ge(dv.h, dv.n)
                            ncp = se.inc(nc.scalar.copy(sgt[:, ssl], ppg[:]))
                            evac_n["se" + str(id(ppg))] = ncp
                            nc.scalar.wait_ge(pe.h, nmmu)
                            nsi = se.inc(nc.scalar.activation(harena[:, hsl], ppu[:], AF.Silu))
                            evac_n["se" + str(id(ppu))] = nsi
                            nc.vector.wait_ge(se.h, nsi)
                            evac_n[id(ppu)] = dv.inc(nc.vector.tensor_tensor(
                                harena[:, hsl], harena[:, hsl], sgt[:, ssl], ALU.mult))
                    ub_last[0] = ub_last[1] = pe.n
                n_h = dv.n

                dw = self.shared_dwT if e == "sh" else self.down_wT[e]
                for cq in range(4):
                    nc.gpsimd.wait_ge(pe.h, pe.n)
                    ins = nc.gpsimd.dma_start(
                        out=dwbuf[:, 0:n_hs * 512].rearrange("p (a o) -> p a o", a=n_hs),
                        in_=dw[:, cq * 512:(cq + 1) * 512].rearrange("(a p) o -> p a o", p=128))
                    w.inc(ins, 16)
                    n_w = w.n
                    for tt in range(NTTH):
                        pp = pdn if (tt % 2 == 0) else pdn2
                        nc.tensor.wait_ge(w.h, n_w)
                        nc.tensor.wait_ge(dv.h, max(n_h, evac_n.get(id(pp), 0)))
                        for hsb in range(n_hs):
                            ins = nc.tensor.matmul(pp[:],
                                                   harena[:, hsb * HALF + tt * 128: hsb * HALF + tt * 128 + 128],
                                                   dwbuf[:, hsb * 512:(hsb + 1) * 512],
                                                   start=(hsb == 0), stop=(hsb == n_hs - 1))
                        nmm = pe.inc(ins)
                        nc.vector.wait_ge(pe.h, nmm)
                        aslice = acc[:, tt * DM + cq * 512: tt * DM + cq * 512 + 512]
                        if e == "sh":
                            ins = nc.vector.tensor_copy(aslice, pp[:])
                        else:
                            ins = nc.vector.tensor_tensor(aslice, aslice, pp[:], ALU.add)
                        evac_n[id(pp)] = dv.inc(ins)
            n_ffn = dv.n

            for tt in range(NTTH):
                gtt = half * NTTH + tt
                for cq in range(4):
                    nc.sync.wait_ge(dv.h, dv.n)
                    nc.sync.dma_start(out=xf1[:], in_=self.d_x1[gtt * 128:(gtt + 1) * 128,
                                                               cq * 512:(cq + 1) * 512]).then_inc(ld.h, 16)
                    ld.n += 16
                    nc.vector.wait_ge(ld.h, ld.n)
                    nc.vector.wait_ge(dv.h, n_ffn)
                    nev = dv.inc(nc.vector.tensor_tensor(xf1[:], acc[:, tt * DM + cq * 512: tt * DM + cq * 512 + 512],
                                                         xf1[:], ALU.add))
                    nc.scalar.wait_ge(dv.h, nev)
                    st.inc(nc.scalar.dma_start(out=self.out_sl[gtt * 128:(gtt + 1) * 128, cq * 512:(cq + 1) * 512],
                                               in_=xf1[:]), 16)
            if "acc" in self.dbgt:
                nc.scalar.wait_ge(dv.h, n_ffn)
                for tt in range(NTTH):
                    st.inc(nc.scalar.dma_start(
                        out=self.dbgt["acc"][(half * NTTH + tt) * 128:(half * NTTH + tt + 1) * 128, :],
                        in_=acc[:, tt * DM:(tt + 1) * DM]), 16)

        nc.sync.wait_ge(st.h, st.n)
        nc.gpsimd.wait_ge(st.h, st.n)


def host_inputs(inp, T=16384):
    import ml_dtypes
    TSL = T // NCORE
    SEGc = min(1024, TSL)
    NSEG = T // SEGc
    f32 = np.float32
    x = np.asarray(inp["L0_attn_x"], f32).reshape(T, DM)
    ident = np.eye(128)
    cm = np.zeros((L, L), f32)
    dq = np.zeros((L, L), f32)
    for s_ in range(L):
        for t_ in range(L):
            if s_ < t_:
                cm[s_, t_] = SCALE * SCALE
            if s_ <= t_:
                dq[s_, t_] = 1.0
    cmask = np.concatenate([cm, cm], axis=1)
    dqmask = np.concatenate([dq, dq], axis=1)
    bf = ml_dtypes.bfloat16
    maps = []
    for r in range(NCORE):
        hsl = slice(r * HPC * D, (r + 1) * HPC * D)
        i_beta = np.array([rr * 16 + r * HPC + hh for hh in range(HPC) for rr in range(NCORE)], np.uint32)
        i_bbc = np.array([((seg * SEGc // TSL) * 16 + r * HPC + hh) * (TSL // SEGc) + (seg * SEGc % TSL) // SEGc
                          for hh in range(HPC) for seg in range(NSEG)], np.uint32)
        i_oh = np.array([(a * 128 + p) * NCORE + r for p in range(128) for a in range(16)], np.uint32)
        NHh = max(1, TSL // 1024)
        i_we = np.zeros(64, np.uint32)
        for e in range(8):
            for hf in range(NHh):
                i_we[e * NHh + hf] = e * NHh + hf
        m = {
            "x_sl": np.ascontiguousarray(x[r * TSL:(r + 1) * TSL]),
            "wq_s": np.ascontiguousarray(np.asarray(inp["L0_attn_Wq"], f32)[:, hsl]),
            "wk_s": np.ascontiguousarray(np.asarray(inp["L0_attn_Wk"], f32)[:, hsl]),
            "wv_s": np.ascontiguousarray(np.asarray(inp["L0_attn_Wv"], f32)[:, hsl]),
            "wa_s": np.ascontiguousarray(np.asarray(inp["L0_attn_Wa"], f32)[:, hsl]),
            "wg_s": np.ascontiguousarray(np.asarray(inp["L0_attn_Wg"], f32)[:, hsl]),
            "ba_s": np.ascontiguousarray(np.asarray(inp["L0_attn_ba"], f32)[hsl]),
            "wb_full": np.ascontiguousarray(np.asarray(inp["L0_attn_Wb"], f32)),
            "bb": np.ascontiguousarray(np.asarray(inp["L0_attn_bb"], f32)),
            "qcw_s": np.ascontiguousarray(np.asarray(inp["L0_attn_qconv_w"], f32)[hsl]),
            "kcw_s": np.ascontiguousarray(np.asarray(inp["L0_attn_kconv_w"], f32)[hsl]),
            "vcw_s": np.ascontiguousarray(np.asarray(inp["L0_attn_vconv_w"], f32)[hsl]),
            "qcb_s": np.ascontiguousarray(np.asarray(inp["L0_attn_qconv_b"], f32)[hsl]),
            "kcb_s": np.ascontiguousarray(np.asarray(inp["L0_attn_kconv_b"], f32)[hsl]),
            "vcb_s": np.ascontiguousarray(np.asarray(inp["L0_attn_vconv_b"], f32)[hsl]),
            "ln_g": np.ascontiguousarray(np.asarray(inp["L0_attn_ln_g"], f32)),
            "ln_b": np.ascontiguousarray(np.asarray(inp["L0_attn_ln_b"], f32)),
            "wo": np.ascontiguousarray(np.asarray(inp["L0_attn_Wo"], f32)),
            "gate_wT": np.ascontiguousarray(np.asarray(inp["L0_ffn_gate_w"], f32).T),
            "gate_b": np.ascontiguousarray(np.asarray(inp["L0_ffn_gate_bias"], f32)),
            "shared_gwT": np.ascontiguousarray(np.asarray(inp["L0_ffn_shared_gw"], f32).T),
            "shared_dwT": np.ascontiguousarray(np.asarray(inp["L0_ffn_shared_dw"], f32).T),
            "up_wT": np.ascontiguousarray(np.asarray(inp["L0_ffn_up_w"], f32).transpose(0, 2, 1)),
            "down_wT": np.ascontiguousarray(np.asarray(inp["L0_ffn_down_w"], f32).transpose(0, 2, 1)),
            "ident_bf": ident.astype(bf),
            "ident_f32": ident.astype(f32),
            "cmask": cmask.astype(bf),
            "dqmask": dqmask.astype(bf),
            "i_beta": i_beta,
            "i_bbc": i_bbc,
            "i_oh": i_oh,
            "i_we": i_we,
        }
        maps.append(m)
    return maps


from concourse.bass_utils import run_bass_kernel_spmd

_CACHED = {}

def kernel(**inputs):
    T = int(np.asarray(inputs["L0_attn_x"]).shape[1])
    if T not in _CACHED:
        b = Builder(T=T)
        _CACHED[T] = b.build()
    nc = _CACHED[T]
    maps = host_inputs(inputs, T=T)
    res = run_bass_kernel_spmd(nc, maps, core_ids=list(range(NCORE)))
    out = np.concatenate([res.results[r]["out_sl"] for r in range(NCORE)], axis=0)
    return out.reshape(1, T, DM).astype(np.float32)


# version 09 (final)
# speedup vs baseline: 1.1975x; 1.1975x over previous
import numpy as np
import contextlib
import concourse.bass as bass
import concourse.mybir as mybir
from concourse.bass import IndirectOffsetOnAxis

F32 = mybir.dt.float32
BF16 = mybir.dt.bfloat16
U32 = mybir.dt.uint32
AF = mybir.ActivationFunctionType
ALU = mybir.AluOpType
AX = mybir.AxisListType

H = 16
D = 128
DM = 2048
KC = 4
NCORE = 8
HPC = 2
SCALE = float(D) ** -0.5
LN_EPS = 1e-5
L = 32
HS = 2816
HHALF = 1408
SHS = 5632
SHH = 2816
NE = 8
BIGNEG = 30000.0


class Sem:
    def __init__(self, nc, name, stack):
        self.h = stack.enter_context(nc.semaphore(name))
        self.n = 0

    def inc(self, instr, k=1):
        instr.then_inc(self.h, k)
        self.n += k
        return self.n


class Builder:
    def __init__(self, T=16384, debug=(), stop_after='full'):
        self.T = T
        self.TSL = T // NCORE
        self.SEG = min(1024, self.TSL)
        assert self.TSL % self.SEG == 0 and self.SEG % L == 0
        self.NSEG = T // self.SEG
        self.NCS = self.SEG // L
        self.NBLK = self.SEG // 128
        self.NLB = self.TSL // 128
        self.debug = set(debug)
        self.stop_after = stop_after
        self.dbgt = {}
        self.nc = bass.Bass("TRN2", target_bir_lowering=False)
        self._stack = contextlib.ExitStack()

    def sb(self, name, shape, dtype, stack=None):
        return (stack or self._stack).enter_context(self.nc.sbuf_tensor(name, shape, dtype))

    def ps(self, name, shape, dtype, stack=None):
        return (stack or self._stack).enter_context(self.nc.psum_tensor(name, shape, dtype))

    def dbg(self, name, shape, dtype=F32):
        if name in self.debug:
            self.dbgt[name] = self.nc.dram_tensor("dbg_" + name, shape, dtype, kind="ExternalOutput").ap()

    def barrier(self):
        nc = self.nc
        sems = [self.s_ld, self.s_w, self.s_pe, self.s_dv, self.s_se, self.s_st, self.s_gp, self.s_cc, self.s_gpc]
        for eng in (nc.tensor, nc.vector, nc.scalar, nc.sync, nc.gpsimd):
            for s in sems:
                if s.n > 0:
                    eng.wait_ge(s.h, s.n)

    def build(self):
        nc = self.nc
        T, TSL = self.T, self.TSL
        dr = lambda n, s, d=F32: nc.dram_tensor(n, s, d, kind="ExternalInput").ap()
        di = lambda n, s, d=F32: nc.dram_tensor(n, s, d, kind="Internal").ap()

        self.x_sl = dr("x_sl", [TSL, DM])
        self.wq_s = dr("wq_s", [DM, HPC * D]); self.wk_s = dr("wk_s", [DM, HPC * D])
        self.wv_s = dr("wv_s", [DM, HPC * D]); self.wa_s = dr("wa_s", [DM, HPC * D])
        self.wg_s = dr("wg_s", [DM, HPC * D])
        self.ba_s = dr("ba_s", [HPC * D]); self.wb_full = dr("wb_full", [DM, H]); self.bb_in = dr("bb", [H])
        self.qcw = dr("qcw_s", [HPC * D, KC]); self.kcw = dr("kcw_s", [HPC * D, KC]); self.vcw = dr("vcw_s", [HPC * D, KC])
        self.qcb = dr("qcb_s", [HPC * D]); self.kcb = dr("kcb_s", [HPC * D]); self.vcb = dr("vcb_s", [HPC * D])
        self.ln_g = dr("ln_g", [D]); self.ln_b = dr("ln_b", [D])
        self.wo = dr("wo", [DM, DM])
        self.gate_wT = dr("gate_wT", [DM, NE]); self.gate_b = dr("gate_b", [NE])
        self.shared_gwT = dr("shared_gwT", [DM, SHS]); self.shared_dwT = dr("shared_dwT", [SHH, DM])
        self.up_wT = dr("up_wT", [NE, DM, HS]); self.down_wT = dr("down_wT", [NE, HHALF, DM])
        self.ident_bf = dr("ident_bf", [128, 128], BF16)
        self.ident_f32 = dr("ident_f32", [128, 128], F32)
        self.cmask = dr("cmask", [L, 2 * L], BF16)
        self.dqmask = dr("dqmask", [L, 2 * L], BF16)
        self.i_beta = dr("i_beta", [16], U32)
        self.i_bbc = dr("i_bbc", [HPC * self.NSEG], U32)
        self.i_oh = dr("i_oh", [DM], U32)
        self.i_we = dr("i_we", [64], U32)

        self.d_xh_in = di("d_xh_in", [DM, TSL], BF16)
        self.d_xh_all = di("d_xh_all", [NCORE * DM, TSL], BF16)
        self.d_beta_in = di("d_beta_in", [16, TSL])
        self.d_beta_all = di("d_beta_all", [NCORE * 16, TSL])
        self.d_oh_in = di("d_oh_in", [HPC * D, T], BF16)
        self.d_oh_all = di("d_oh_all", [NCORE * HPC * D, T], BF16)
        self.d_x1 = di("d_x1", [TSL, DM])
        self.d_x2T = di("d_x2T", [DM, TSL], BF16)
        self.d_w8T = di("d_w8T", [NE, TSL])

        self.out_sl = nc.dram_tensor("out_sl", [TSL, DM], F32, kind="ExternalOutput").ap()

        self.dbg("xhat", [DM, TSL], BF16)
        self.dbg("beta", [16, TSL])
        self.dbg("qc", [HPC * D, T], BF16)
        self.dbg("kc", [HPC * D, T], BF16)
        self.dbg("vc", [HPC * D, T], BF16)
        self.dbg("ac", [HPC * D, T])
        self.dbg("oraw", [HPC * D, T])
        self.dbg("ohat", [HPC * D, T], BF16)
        self.dbg("x1", [TSL, DM])
        self.dbg("x2T", [DM, TSL], BF16)
        self.dbg("gates", [TSL, NE])
        self.dbg("acc", [TSL, DM])

        ss = self._stack
        for nm in ["s_ld", "s_w", "s_pe", "s_dv", "s_se", "s_st", "s_cc", "s_gp", "s_gpc"]:
            setattr(self, nm, Sem(nc, nm, ss))

        with nc.allow_low_precision(reason="bf16 ops blessed by stated tolerance"), nc.Block() as block:
            self.block = block
            self.emit_consts()
            with contextlib.ExitStack() as ph12:
                self.ph12 = ph12
                with contextlib.ExitStack() as ph1:
                    self.ph1 = ph1
                    self.emit_phase1()
                if self.stop_after != "p1":
                    self.emit_gather1()
                    if self.stop_after != "g1":
                        self.emit_phase2()
                        if not self.stop_after.startswith("p2"):
                            self.emit_gather2()
            if self.stop_after in ("full",):
                self.barrier()
                with contextlib.ExitStack() as ph3:
                    self.ph3 = ph3
                    with contextlib.ExitStack() as ph3a:
                        self.ph3a = ph3a
                        self.emit_phase3a()
                    self.barrier()
                    self.emit_phase3b()
            else:
                self.barrier()
        return nc

    def emit_consts(self):
        nc = self.nc
        ld, gp, dv = self.s_ld, self.s_gp, self.s_dv
        self.c_ident_bf = self.sb("c_ident_bf", [128, 128], BF16)
        self.c_ident_f32 = self.sb("c_ident_f32", [128, 128], F32)
        self.c_cmask = self.sb("c_cmask", [L, 2 * L], BF16)
        self.c_dqmask = self.sb("c_dqmask", [L, 2 * L], BF16)
        self.c_lng = self.sb("c_lng", [128, 1], F32)
        self.c_lnb = self.sb("c_lnb", [128, 1], F32)
        self.c_ba = self.sb("c_ba", [128, HPC], F32)
        self.c_cb = self.sb("c_cb", [128, 3 * HPC], F32)
        self.c_wb = self.sb("c_wb", [128, 16 * H], BF16)
        self.c_bb = self.sb("c_bb", [1, H], BF16)
        self.c_gateb = self.sb("c_gateb", [128, NE], F32)
        self.c_ones_bf = self.sb("c_ones_bf", [1, 128], BF16)
        self.c_onesf = self.sb("c_onesf", [128, 128], F32)
        self.c_convw = self.sb("c_convw", [128, 3 * HPC * KC], F32)
        self.c_convd = self.sb("c_convd", [128, 3 * HPC * KC * 128], BF16)

        S = nc.sync
        S.dma_start(out=self.c_ident_bf[:], in_=self.ident_bf).then_inc(ld.h, 16); ld.n += 16
        S.dma_start(out=self.c_ident_f32[:], in_=self.ident_f32).then_inc(ld.h, 16); ld.n += 16
        S.dma_start(out=self.c_cmask[:], in_=self.cmask).then_inc(ld.h, 16); ld.n += 16
        S.dma_start(out=self.c_dqmask[:], in_=self.dqmask).then_inc(ld.h, 16); ld.n += 16
        S.dma_start(out=self.c_lng[:], in_=self.ln_g.unsqueeze(1)).then_inc(ld.h, 16); ld.n += 16
        S.dma_start(out=self.c_lnb[:], in_=self.ln_b.unsqueeze(1)).then_inc(ld.h, 16); ld.n += 16
        S.dma_start(out=self.c_ba[:], in_=self.ba_s.rearrange("(h c) -> c h", h=HPC), allow_slow_non_contiguous=True).then_inc(ld.h, 16); ld.n += 16
        for j, t in enumerate([self.qcb, self.kcb, self.vcb]):
            S.dma_start(out=self.c_cb[:, j * HPC:(j + 1) * HPC],
                        in_=t.rearrange("(h c) -> c h", h=HPC), allow_slow_non_contiguous=True).then_inc(ld.h, 16); ld.n += 16
        for j, t in enumerate([self.qcw, self.kcw, self.vcw]):
            for hh in range(HPC):
                S.dma_start(out=self.c_convw[:, j * HPC * KC + hh * KC: j * HPC * KC + (hh + 1) * KC],
                            in_=t[hh * 128:(hh + 1) * 128, :]).then_inc(ld.h, 16); ld.n += 16
        S.dma_start(out=self.c_gateb[:],
                    in_=self.gate_b.unsqueeze(0).partition_broadcast(128).squeeze(1)).then_inc(ld.h, 16); ld.n += 16
        nc.gpsimd.dma_start(out=self.c_wb[:].rearrange("p (a h) -> p a h", a=16),
                            in_=self.wb_full.rearrange("(a p) h -> p a h", p=128)).then_inc(gp.h, 16); gp.n += 16
        nc.gpsimd.dma_start(out=self.c_bb[:], in_=self.bb_in.unsqueeze(0)).then_inc(gp.h, 16); gp.n += 16
        nc.gpsimd.memset(self.c_ones_bf[:], 1.0)
        self.s_gpc.inc(nc.gpsimd.memset(self.c_onesf[:], 1.0), 1)
        nc.vector.wait_ge(ld.h, ld.n)
        for j in range(3 * HPC * KC):
            ins = nc.vector.tensor_scalar(self.c_convd[:, j * 128:(j + 1) * 128],
                                          self.c_ident_f32[:], self.c_convw[:, j:j + 1], None, ALU.mult)
        dv.inc(ins)
        self.n_consts_dv = dv.n
        self.n_consts_gp = gp.n
        self.n_consts_gpc = self.s_gpc.n

    def emit_ln_tile(self, xti, xc, xsq, stat, xh):
        nc = self.nc
        dv, se = self.s_dv, self.s_se
        n = dv.inc(nc.vector.tensor_reduce(stat[:, 0:1], xti[:], AX.X, ALU.add))
        nc.vector.wait_ge(dv.h, n)
        n = dv.inc(nc.vector.tensor_scalar(stat[:, 1:2], stat[:, 0:1], -1.0 / DM, None, ALU.mult))
        nc.vector.wait_ge(dv.h, n)
        n = dv.inc(nc.vector.tensor_scalar(xc[:], xti[:], stat[:, 1:2], None, ALU.add))
        nc.scalar.wait_ge(dv.h, n)
        n = se.inc(nc.scalar.activation(xsq[:], xc[:], AF.Square))
        nc.vector.wait_ge(se.h, n)
        n = dv.inc(nc.vector.tensor_reduce(stat[:, 2:3], xsq[:], AX.X, ALU.add))
        nc.vector.wait_ge(dv.h, n)
        n = dv.inc(nc.vector.tensor_scalar(stat[:, 2:3], stat[:, 2:3], 1.0 / DM, LN_EPS, ALU.mult, ALU.add))
        nc.scalar.wait_ge(dv.h, n)
        n = se.inc(nc.scalar.activation(stat[:, 2:3], stat[:, 2:3], AF.Sqrt))
        nc.vector.wait_ge(se.h, n)
        n = dv.inc(nc.vector.reciprocal(stat[:, 3:4], stat[:, 2:3]))
        nc.vector.wait_ge(dv.h, n)
        return dv.inc(nc.vector.tensor_scalar(xh[:], xc[:], stat[:, 3:4], None, ALU.mult))

    def emit_phase1(self):
        nc = self.nc
        TSL = self.TSL
        NTT = TSL // 128
        ld, pe, dv, se, st = self.s_ld, self.s_pe, self.s_dv, self.s_se, self.s_st
        sb = lambda n, s_, d_: self.sb(n, s_, d_, self.ph1)

        xti = sb("p1_x", [128, DM], F32)
        xc = sb("p1_xc", [128, DM], F32)
        xsq = sb("p1_xsq", [128, DM], F32)
        stat = sb("p1_stat", [128, 4], F32)
        xh = sb("p1_xh", [128, DM], BF16)
        xhT = sb("p1_xhT", [128, 16 * 128], BF16)
        sbeta = sb("p1_beta", [128, 16], F32)
        sbetaT = sb("p1_betaT", [16, 128], F32)
        self.pmm = self.ps("ps_mm", [128, 512], F32, self.ph12)
        self.pmm2 = self.ps("ps_mm2", [128, 512], F32, self.ph12)
        self.pmisc = self.ps("ps_misc", [128, 512], F32, self.ph12)
        self.pbig = self.pmisc[:, 0:256]
        self.pcdF = self.ps("ps_cd", [64, 256], F32, self.ph12)
        self.pin = self.pmisc[:, 256:384]
        self.ptt = self.ps("ps_tt", [128, 128], BF16, self.ph12)
        self.pttf = self.ps("ps_ttf", [128, 128], F32, self.ph12)
        self.pzh1a = self.ps("ps_zh1a", [128, 512], F32, self.ph12)
        self.pzh1b = self.ps("ps_zh1b", [128, 512], F32, self.ph12)
        ptb = self.ptt
        ptf = self.pttf
        pb8 = self.pmisc[:, 256:272]

        st_prev = 0
        for tt in range(NTT):
            nc.sync.wait_ge(st.h, st_prev)
            nc.sync.wait_ge(dv.h, dv.n)
            nc.sync.dma_start(out=xti[:], in_=self.x_sl[tt * 128:(tt + 1) * 128, :]).then_inc(ld.h, 16); ld.n += 16
            nc.vector.wait_ge(ld.h, ld.n)
            nxh = self.emit_ln_tile(xti, xc, xsq, stat, xh)
            for cb in range(16):
                nc.tensor.wait_ge(dv.h, nxh if cb == 0 else dv.n)
                nmm = pe.inc(nc.tensor.transpose(ptb[:], xh[:, cb * 128:(cb + 1) * 128], self.c_ident_bf[:]))
                nc.vector.wait_ge(pe.h, nmm)
                dv.inc(nc.vector.tensor_copy(xhT[:, cb * 128:(cb + 1) * 128], ptb[:]))
            ntr = dv.n
            nc.tensor.wait_ge(dv.h, ntr)
            nc.tensor.wait_ge(self.s_gp.h, self.n_consts_gp)
            nc.tensor.wait_ge(self.s_gpc.h, self.n_consts_gpc)
            for cb in range(16):
                nc.tensor.matmul(pb8, xhT[:, cb * 128:(cb + 1) * 128],
                                 self.c_wb[:, cb * 16:(cb + 1) * 16], start=(cb == 0), stop=False)
            nmm = pe.inc(nc.tensor.matmul(pb8, self.c_ones_bf[:], self.c_bb[:], start=False, stop=True))
            nc.scalar.wait_ge(pe.h, nmm)
            nsg = se.inc(nc.scalar.activation(sbeta[:], pb8, AF.Sigmoid))
            nc.tensor.wait_ge(se.h, nsg)
            nmm = pe.inc(nc.tensor.transpose(ptf[0:16, :], sbeta[:], self.c_ident_f32[:]))
            nc.vector.wait_ge(pe.h, nmm)
            nev = dv.inc(nc.vector.tensor_copy(sbetaT[:], ptf[0:16, :]))
            nc.scalar.wait_ge(dv.h, nev)
            for cb in range(16):
                st.inc(nc.scalar.dma_start(out=self.d_xh_in[cb * 128:(cb + 1) * 128, tt * 128:(tt + 1) * 128],
                                           in_=xhT[:, cb * 128:(cb + 1) * 128]), 16)
            st.inc(nc.scalar.dma_start(out=self.d_beta_in[:, tt * 128:(tt + 1) * 128], in_=sbetaT[:]), 16)
            st_prev = st.n
        self.p1_store_n = st.n

    def emit_gather1(self):
        nc = self.nc
        cc, st, gp = self.s_cc, self.s_st, self.s_gp
        nc.gpsimd.wait_ge(st.h, self.p1_store_n)
        cc.inc(nc.gpsimd.collective_compute("AllGather", ALU.bypass, [list(range(NCORE))],
                                            [self.d_xh_in.opt()], [self.d_xh_all.opt()]), 1)
        cc.inc(nc.gpsimd.collective_compute("AllGather", ALU.bypass, [list(range(NCORE))],
                                            [self.d_beta_in.opt()], [self.d_beta_all.opt()]), 1)
        self.g1_cc = cc.n
        if "xhat" in self.debug:
            gp.inc(nc.gpsimd.dma_start(out=self.dbgt["xhat"], in_=self.d_xh_in), 16)
        if "beta" in self.debug:
            gp.inc(nc.gpsimd.dma_start(out=self.dbgt["beta"], in_=self.d_beta_in), 16)

    def emit_phase2(self):
        nc = self.nc
        T, TSL, SEG, NSEG, NCS, NBLK = self.T, self.TSL, self.SEG, self.NSEG, self.NCS, self.NBLK
        ld, pe, dv, se, st, gp, w, cc = (self.s_ld, self.s_pe, self.s_dv, self.s_se,
                                         self.s_st, self.s_gp, self.s_w, self.s_cc)
        sb = lambda n, s_, d_: self.sb(n, s_, d_, self.ph12)
        NQ = SEG // 512

        wproj = sb("p2_wproj", [128, 5 * 16 * HPC * 128], BF16)
        wsrc = [self.wq_s, self.wk_s, self.wv_s, self.wa_s, self.wg_s]
        nc.gpsimd.wait_ge(cc.h, self.g1_cc)
        for p in range(5):
            for icb in range(16):
                ins = nc.gpsimd.dma_start(
                    out=wproj[:, ((p * 16 + icb) * HPC) * 128: ((p * 16 + icb) * HPC + HPC) * 128],
                    in_=wsrc[p][icb * 128:(icb + 1) * 128, :])
                w.inc(ins, 16)
        self.p2_w_n = w.n

        def wtile(p, icb, hh):
            off = ((p * 16 + icb) * HPC + hh) * 128
            return wproj[:, off:off + 128]

        beta_own = sb("p2_beta_own", [16, TSL], F32)
        i_beta_t = sb("p2_ibeta", [16, 1], U32)
        betaT = sb("p2_betaT", [128, self.NLB * 16], F32)
        ibbc_t = sb("p2_ibbc", [128, 1], U32)

        nc.sync.dma_start(out=i_beta_t[:], in_=self.i_beta.unsqueeze(1)).then_inc(ld.h, 16); ld.n += 16
        nc.gpsimd.wait_ge(cc.h, self.g1_cc)
        nc.gpsimd.wait_ge(ld.h, ld.n)
        nbo = gp.inc(nc.gpsimd.indirect_dma_start(out=beta_own[:], out_offset=None, in_=self.d_beta_all,
                                                  in_offset=IndirectOffsetOnAxis(ap=i_beta_t[:, 0:1], axis=0)), 16)
        ptf2 = self.pttf
        for lb in range(self.NLB):
            nc.tensor.wait_ge(gp.h, nbo)
            if lb > 0:
                nc.tensor.wait_ge(dv.h, dv.n)
            nmm = pe.inc(nc.tensor.transpose(ptf2[0:128, 0:16], beta_own[:, lb * 128:(lb + 1) * 128],
                                             self.c_ident_f32[0:16, 0:16]))
            nc.vector.wait_ge(pe.h, nmm)
            dv.inc(nc.vector.tensor_copy(betaT[:, lb * 16:(lb + 1) * 16], ptf2[0:128, 0:16]))

        s_cm = sb("p2_scm", [128, HPC * 128], BF16)
        s_km = sb("p2_skm", [128, HPC * 128], BF16)
        nc.vector.memset(s_cm[:], 0.0)
        dv.inc(nc.vector.memset(s_km[:], 0.0))

        xh_seg = sb("p2_xh", [128, 16 * SEG], BF16)
        qpad = sb("p2_qpad", [128, HPC * (SEG + 3)], BF16)
        kpad = sb("p2_kpad", [128, HPC * (SEG + 3)], BF16)
        vpad = sb("p2_vpad", [128, HPC * (SEG + 3)], BF16)
        a_cm = sb("p2_acm", [128, HPC * SEG], F32)
        g_cm = sb("p2_gcm", [128, HPC * SEG], BF16)
        q_cm = sb("p2_qcm", [128, HPC * SEG], BF16)
        k_cm = sb("p2_kcm", [128, HPC * SEG], BF16)
        v_cm = sb("p2_vcm", [128, HPC * SEG], BF16)
        ktb_cm = sb("p2_ktb", [128, HPC * SEG], BF16)
        bbc = sb("p2_bbc", [128, SEG], BF16)
        p_cm = sb("p2_pcm", [128, HPC * SEG], F32)
        ktm = sb("p2_ktm", [L, HPC * NCS * 128], BF16)
        vtm = sb("p2_vtm", [128, NBLK * HPC * 128], BF16)
        ratm = sb("p2_ratm", [128, NBLK * HPC * 128], BF16)
        rtm = sb("p2_rtm", [128, NBLK * HPC * 128], BF16)
        o_cm = sb("p2_ocm", [128, HPC * SEG], F32)
        osq = sb("p2_osq", [128, SEG], F32)
        orow = sb("p2_orow", [128, SEG], F32)
        orow2 = sb("p2_orow2", [128, SEG], F32)
        obc = sb("p2_obc", [128, SEG], F32)
        obc2 = sb("p2_obc2", [128, SEG], F32)
        oh_seg = sb("p2_ohseg", [128, HPC * SEG], BF16)

        bhat = sb("p2_bhat", [L, 2 * 128], BF16)
        t_b1 = sb("p2_tb1", [L, 2 * 128], F32)
        t_b2 = sb("p2_tb2", [L, 2 * 128], F32)
        t1t = sb("p2_t1", [L, 2 * 128], BF16)
        yfin = t_b1
        w_tm = sb("p2_wtm", [L, 2 * 128], BF16)
        ct_sb = sb("p2_ctsb", [L, 2 * L], BF16)
        dq_sb = sb("p2_dqsb", [L, 2 * L], BF16)
        diagp = sb("p2_diagp", [128, 128], BF16)
        inner_cm = sb("p2_inner", [128, 128], BF16)

        pmm, pmm2 = self.pmm, self.pmm2
        pz0 = self.pmm[0:L, 0:256]
        pz1 = self.pmm2[0:L, 0:256]
        pks = self.pbig[0:L, 0:256]
        pcd = self.pcdF[0:L, 0:2 * L]
        pcd2 = self.pcdF[0:L, 128:128 + 2 * L]
        po = self.pbig[:, 0:2 * L]
        pin = self.pin
        pkm = self.pin
        ptt = self.ptt
        pttf = self.pttf
        pst = self.pmm[0:1, 0:512]
        evac_n = {}

        stage = self.stop_after
        if stage == 'p2a':
            self.p2_store_n = st.n
            return
        seg_limit = 1 if (stage.startswith("p2") and stage != "p2") else NSEG
        if stage.startswith('p2s') and len(stage) > 3:
            seg_limit = int(stage[3:])
        for seg in range(seg_limit):
            t0 = seg * SEG
            rblk = t0 // TSL
            tloc = t0 - rblk * TSL
            self.barrier()

            for icb in range(16):
                nc.sync.dma_start(out=xh_seg[:, icb * SEG:(icb + 1) * SEG],
                                  in_=self.d_xh_all[rblk * DM + icb * 128: rblk * DM + (icb + 1) * 128,
                                                    tloc:tloc + SEG]).then_inc(ld.h, 16)
                ld.n += 16
            n_xh = ld.n

            for buf in (qpad, kpad, vpad):
                for hh in range(HPC):
                    base = hh * (SEG + 3)
                    if seg == 0:
                        ins = nc.vector.memset(buf[:, base:base + 3], 0.0)
                    else:
                        ins = nc.vector.tensor_copy(buf[:, base:base + 3], buf[:, base + SEG:base + SEG + 3])
            n_pad = dv.inc(ins)

            for p, dst in [(0, qpad), (1, kpad), (2, vpad), (3, a_cm), (4, g_cm)]:
                for hh in range(HPC):
                    for th in range(NQ):
                        pp = pmm if (th % 2 == 0) else pmm2
                        nc.tensor.wait_ge(ld.h, n_xh)
                        nc.tensor.wait_ge(self.s_w.h, self.p2_w_n)
                        nc.tensor.wait_ge(se.h, evac_n.get(id(pp), 0))
                        for icb in range(16):
                            ins = nc.tensor.matmul(pp[:], wtile(p, icb, hh),
                                                   xh_seg[:, icb * SEG + th * 512: icb * SEG + th * 512 + 512],
                                                   start=(icb == 0), stop=(icb == 15))
                        nmm = pe.inc(ins)
                        nc.scalar.wait_ge(pe.h, nmm)
                        if p <= 2:
                            out = dst[:, hh * (SEG + 3) + 3 + th * 512: hh * (SEG + 3) + 3 + th * 512 + 512]
                            ins = nc.scalar.copy(out, pp[:])
                        elif p == 3:
                            out = dst[:, hh * SEG + th * 512: hh * SEG + th * 512 + 512]
                            ins = nc.scalar.activation(out, pp[:], AF.Sigmoid, bias=self.c_ba[:, hh:hh + 1])
                        else:
                            out = dst[:, hh * SEG + th * 512: hh * SEG + th * 512 + 512]
                            ins = nc.scalar.activation(out, pp[:], AF.Sigmoid)
                        evac_n[id(pp)] = se.inc(ins)
            n_proj = se.n
            if stage == 'p2b':
                continue

            for p, (src, dst) in enumerate([(qpad, q_cm), (kpad, k_cm), (vpad, v_cm)]):
                for hh in range(HPC):
                    for th in range(NQ):
                        pp = pmm if (th % 2 == 0) else pmm2
                        nc.tensor.wait_ge(se.h, n_proj)
                        nc.tensor.wait_ge(dv.h, max(n_pad, self.n_consts_dv))
                        nc.tensor.wait_ge(se.h, evac_n.get(id(pp), 0))
                        for j in range(KC):
                            coff = ((p * HPC + hh) * KC + j) * 128
                            moff = hh * (SEG + 3) + th * 512 + j
                            ins = nc.tensor.matmul(pp[:], self.c_convd[:, coff:coff + 128],
                                                   src[:, moff:moff + 512],
                                                   start=(j == 0), stop=(j == KC - 1))
                        nmm = pe.inc(ins)
                        nc.scalar.wait_ge(pe.h, nmm)
                        out = dst[:, hh * SEG + th * 512: hh * SEG + th * 512 + 512]
                        ins = nc.scalar.activation(out, pp[:], AF.Silu,
                                                   bias=self.c_cb[:, p * HPC + hh:p * HPC + hh + 1])
                        evac_n[id(pp)] = se.inc(ins)
            n_conv = se.n
            if stage == 'p2c':
                continue

            for nm, srcb in [("qc", q_cm), ("kc", k_cm), ("vc", v_cm), ("ac", a_cm)]:
                if nm in self.dbgt:
                    nc.scalar.wait_ge(se.h, n_conv)
                    for hh in range(HPC):
                        st.inc(nc.scalar.dma_start(
                            out=self.dbgt[nm][hh * 128:(hh + 1) * 128, t0:t0 + SEG],
                            in_=srcb[:, hh * SEG:(hh + 1) * SEG]), 16)

            for hh in range(HPC):
                nc.sync.wait_ge(dv.h, dv.n)
                nc.sync.dma_start(out=ibbc_t[:],
                                  in_=self.i_bbc[hh * NSEG + seg: hh * NSEG + seg + 1]
                                  .unsqueeze(0).partition_broadcast(128).squeeze(1)).then_inc(ld.h, 16)
                ld.n += 16
                nc.gpsimd.wait_ge(ld.h, ld.n)
                ngb = gp.inc(nc.gpsimd.indirect_dma_start(
                    out=bbc[:], out_offset=None,
                    in_=self.d_beta_all.rearrange("r (s t) -> (r s) t", s=TSL // SEG),
                    in_offset=IndirectOffsetOnAxis(ap=ibbc_t[:, 0:1], axis=0)), 16)
                nc.vector.wait_ge(gp.h, ngb)
                nc.vector.wait_ge(se.h, n_conv)
                dv.inc(nc.vector.tensor_tensor(ktb_cm[:, hh * SEG:(hh + 1) * SEG],
                                               k_cm[:, hh * SEG:(hh + 1) * SEG], bbc[:], ALU.mult))
            n_ktb = dv.n

            nc.vector.wait_ge(se.h, n_proj)
            nc.vector.wait_ge(pe.h, pe.n)
            for hh in range(HPC):
                for j in range(NCS):
                    c0 = hh * SEG + j * L
                    ins = nc.vector.tensor_tensor_scan(p_cm[:, c0:c0 + L], a_cm[:, c0:c0 + L],
                                                       a_cm[:, c0:c0 + L], 1.0, ALU.mult, ALU.bypass)
            n_pscan = dv.inc(ins)

            for blk in range(NBLK):
                for hh in range(HPC):
                    cbase = hh * SEG + blk * 128
                    fr = (blk * HPC + hh) * 128
                    nc.tensor.wait_ge(se.h, n_conv)
                    nc.tensor.wait_ge(dv.h, dv.n)
                    nmm = pe.inc(nc.tensor.transpose(ptt[:], k_cm[:, cbase:cbase + 128], self.c_ident_bf[:]))
                    nc.vector.wait_ge(pe.h, nmm)
                    j0 = blk * (128 // L)
                    for cc in range(128 // L):
                        ins = nc.vector.tensor_copy(
                            ktm[:, (hh * NCS + j0 + cc) * 128:(hh * NCS + j0 + cc) * 128 + 128],
                            ptt[cc * L:(cc + 1) * L, :])
                    dv.inc(ins)
                    nc.tensor.wait_ge(dv.h, dv.n)
                    nmm = pe.inc(nc.tensor.transpose(ptt[:], v_cm[:, cbase:cbase + 128], self.c_ident_bf[:]))
                    nc.vector.wait_ge(pe.h, nmm)
                    dv.inc(nc.vector.tensor_copy(vtm[:, fr:fr + 128], ptt[:]))
                    nc.tensor.wait_ge(dv.h, dv.n)
                    nmm = pe.inc(nc.tensor.transpose(pttf[:], a_cm[:, cbase:cbase + 128], self.c_ident_f32[:]))
                    nc.vector.wait_ge(pe.h, nmm)
                    dv.inc(nc.vector.reciprocal(ratm[:, fr:fr + 128], pttf[:]))
                    nc.tensor.wait_ge(dv.h, max(dv.n, n_pscan))
                    nmm = pe.inc(nc.tensor.transpose(pttf[:], p_cm[:, cbase:cbase + 128], self.c_ident_f32[:]))
                    nc.vector.wait_ge(pe.h, nmm)
                    dv.inc(nc.vector.reciprocal(rtm[:, fr:fr + 128], pttf[:]))
            n_tm = dv.n
            if stage == 'p2d':
                continue

            if stage.startswith('p2c') and len(stage) > 3:
                ncs_limit = int(stage[3:])
            else:
                ncs_limit = 1 if stage.startswith('p2e') else NCS
            for j in range(ncs_limit):
                tglob = t0 + j * L
                gb = tglob // 128
                lb = gb % self.NLB
                rr = tglob // TSL
                NPB = 128 // L
                poff = L * (j % NPB)
                blk = j // NPB
                tcol = lambda hh: hh * SEG + j * L
                frs = slice((blk * HPC) * 128, (blk * HPC + 2) * 128)

                nc.tensor.wait_ge(dv.h, dv.n)
                nc.tensor.wait_ge(se.h, n_conv)
                for hh in range(HPC):
                    ins = nc.tensor.matmul(pks[:, hh * 128:(hh + 1) * 128],
                                           k_cm[:, tcol(hh):tcol(hh) + L],
                                           s_km[:, hh * 128:(hh + 1) * 128],
                                           start=True, stop=True)
                n_ks = pe.inc(ins)

                for hh in range(HPC):
                    nc.tensor.matmul(pcd[:, hh * L:(hh + 1) * L],
                                     ktb_cm[:, tcol(hh):tcol(hh) + L],
                                     k_cm[:, tcol(hh):tcol(hh) + L],
                                     start=True, stop=True)
                for hh in range(HPC):
                    ins = nc.tensor.matmul(pcd2[:, hh * L:(hh + 1) * L],
                                           k_cm[:, tcol(hh):tcol(hh) + L],
                                           q_cm[:, tcol(hh):tcol(hh) + L],
                                           start=True, stop=True)
                n_cd = pe.inc(ins)

                ra_sl = ratm[poff:poff + L, frs]
                rt_sl = rtm[poff:poff + L, frs]
                vt_sl = vtm[poff:poff + L, frs]
                nc.vector.wait_ge(pe.h, n_ks)
                nc.vector.tensor_tensor(t_b2[:], rt_sl, vt_sl, ALU.mult)
                n1 = dv.inc(nc.vector.scalar_tensor_tensor(t_b1[:], pks[:], SCALE, ra_sl, ALU.mult, ALU.mult))
                nc.vector.wait_ge(dv.h, n1)
                nbh = dv.inc(nc.vector.tensor_tensor(bhat[:], t_b1[:], t_b2[:], ALU.subtract))

                nc.vector.wait_ge(pe.h, n_cd)
                nc.vector.tensor_tensor(ct_sb[:], pcd[:], self.c_cmask[:], ALU.mult)
                nmask = dv.inc(nc.vector.tensor_tensor(dq_sb[:], pcd2[:], self.c_dqmask[:], ALU.mult))
                if stage == 'p2e1':
                    continue

                zb = [[pz0[:, 0:128], pz1[:, 0:128]],
                      [self.pzh1a[0:L, 0:128], self.pzh1b[0:L, 0:128]]]
                last_t1 = [max(nbh, nmask), max(nbh, nmask)]
                last_mm = [0, 0]
                for m in range(1, L):
                    for hh in range(HPC):
                        pzh = zb[hh][m % 2]
                        nc.tensor.wait_ge(dv.h, last_t1[hh])
                        ins = nc.tensor.matmul(pzh,
                                               ct_sb[:, hh * L:(hh + 1) * L],
                                               bhat[:, hh * 128:(hh + 1) * 128],
                                               start=True, stop=(m == 1))
                        if m > 1:
                            ins = nc.tensor.matmul(pzh,
                                                   ct_sb[:, hh * L:(hh + 1) * L],
                                                   t1t[:, hh * 128:(hh + 1) * 128],
                                                   start=False, stop=True)
                        last_mm[hh] = pe.inc(ins)
                    for hh in range(HPC):
                        pzh = zb[hh][m % 2]
                        nc.vector.wait_ge(pe.h, last_mm[hh])
                        last_t1[hh] = dv.inc(nc.vector.scalar_tensor_tensor(
                            t1t[:, hh * 128:(hh + 1) * 128], pzh, -1.0,
                            ratm[poff:poff + L, (blk * HPC + hh) * 128:(blk * HPC + hh) * 128 + 128],
                            ALU.mult, ALU.mult))

                if stage == 'p2e2':
                    continue
                nc.vector.wait_ge(dv.h, max(last_t1))
                nyf = dv.inc(nc.vector.tensor_tensor(yfin[:], bhat[:], t1t[:], ALU.add))
                nc.vector.wait_ge(dv.h, nyf)
                for hh in range(HPC):
                    bcol = betaT[poff:poff + L, lb * 16 + hh * 8 + rr: lb * 16 + hh * 8 + rr + 1]
                    ins = nc.vector.tensor_scalar(w_tm[:, hh * 128:(hh + 1) * 128],
                                                  yfin[:, hh * 128:(hh + 1) * 128],
                                                  bcol, -SCALE, ALU.mult, ALU.mult)
                nwtm = dv.inc(ins)

                if stage == 'p2e3':
                    continue
                nc.tensor.wait_ge(dv.h, nwtm)
                for hh in range(HPC):
                    nc.tensor.matmul(po[:, hh * L:(hh + 1) * L],
                                     s_km[:, hh * 128:(hh + 1) * 128],
                                     q_cm[:, tcol(hh):tcol(hh) + L],
                                     start=True, stop=False)
                    ins = nc.tensor.matmul(po[:, hh * L:(hh + 1) * L],
                                           w_tm[:, hh * 128:(hh + 1) * 128],
                                           dq_sb[:, hh * L:(hh + 1) * L],
                                           start=False, stop=True)
                n_po = pe.inc(ins)
                nc.vector.wait_ge(pe.h, n_po)
                for hh in range(HPC):
                    ins = nc.vector.tensor_tensor(o_cm[:, tcol(hh):tcol(hh) + L],
                                                  p_cm[:, tcol(hh):tcol(hh) + L],
                                                  po[:, hh * L:(hh + 1) * L], ALU.mult)
                dv.inc(ins)

                if stage == 'p2e4':
                    continue
                for hh in range(HPC):
                    nc.tensor.wait_ge(dv.h, dv.n)
                    nc.tensor.matmul(pin[:], self.c_ident_bf[:], s_cm[:, hh * 128:(hh + 1) * 128],
                                     start=True, stop=False)
                    ins = nc.tensor.matmul(pin[:], w_tm[:, hh * 128:(hh + 1) * 128],
                                           ktm[:, (hh * NCS + j) * 128:(hh * NCS + j) * 128 + 128],
                                           start=False, stop=True)
                    nin = pe.inc(ins)
                    nc.vector.wait_ge(pe.h, nin)
                    nev = dv.inc(nc.vector.tensor_copy(inner_cm[:], pin[:]))
                    pcol = p_cm[:, tcol(hh) + L - 1: tcol(hh) + L]
                    nc.vector.wait_ge(dv.h, nev)
                    nc.vector.tensor_scalar(s_cm[:, hh * 128:(hh + 1) * 128], inner_cm[:], pcol, None, ALU.mult)
                    ndg = dv.inc(nc.vector.tensor_scalar(diagp[:], self.c_ident_bf[:], pcol, None, ALU.mult))
                    nc.tensor.wait_ge(dv.h, ndg)
                    nkm = pe.inc(nc.tensor.matmul(pkm[:], inner_cm[:], diagp[:], start=True, stop=True))
                    nc.vector.wait_ge(pe.h, nkm)
                    dv.inc(nc.vector.tensor_copy(s_km[:, hh * 128:(hh + 1) * 128], pkm[:]))

            n_scan = dv.n
            if stage.startswith('p2c') or stage.startswith('p2e'):
                continue
            if "oraw" in self.dbgt:
                nc.scalar.wait_ge(dv.h, n_scan)
                for hh in range(HPC):
                    st.inc(nc.scalar.dma_start(
                        out=self.dbgt["oraw"][hh * 128:(hh + 1) * 128, t0:t0 + SEG],
                        in_=o_cm[:, hh * SEG:(hh + 1) * SEG]), 16)
            for hh in range(HPC):
                nc.scalar.wait_ge(dv.h, n_scan)
                nc.scalar.wait_ge(dv.h, dv.n)
                nsq = se.inc(nc.scalar.activation(osq[:], o_cm[:, hh * SEG:(hh + 1) * SEG], AF.Square))
                for q in range(NQ):
                    nc.tensor.wait_ge(dv.h, dv.n)
                    nmm = pe.inc(nc.tensor.matmul(pst, self.c_onesf[:, 0:1],
                                                  o_cm[:, hh * SEG + q * 512: hh * SEG + q * 512 + 512],
                                                  start=True, stop=True))
                    nc.vector.wait_ge(pe.h, nmm)
                    nev = dv.inc(nc.vector.tensor_copy(orow[0:1, q * 512: q * 512 + 512], pst))
                    nc.tensor.wait_ge(se.h, nsq)
                    nc.tensor.wait_ge(dv.h, nev)
                    nmm = pe.inc(nc.tensor.matmul(pst, self.c_onesf[:, 0:1],
                                                  osq[:, q * 512: q * 512 + 512], start=True, stop=True))
                    nc.vector.wait_ge(pe.h, nmm)
                    dv.inc(nc.vector.tensor_copy(orow2[0:1, q * 512: q * 512 + 512], pst))
                nc.vector.wait_ge(dv.h, dv.n)
                n0 = dv.inc(nc.vector.tensor_scalar(orow[0:1, :], orow[0:1, :], 1.0 / D, None, ALU.mult))
                nc.vector.wait_ge(dv.h, n0)
                n1 = dv.inc(nc.vector.tensor_scalar(orow2[0:1, :], orow2[0:1, :], 1.0 / D, None, ALU.mult))
                nc.vector.wait_ge(dv.h, n1)
                n2 = dv.inc(nc.vector.tensor_tensor(osq[0:1, :], orow[0:1, :], orow[0:1, :], ALU.mult))
                nc.vector.wait_ge(dv.h, n2)
                n3 = dv.inc(nc.vector.tensor_tensor(orow2[0:1, :], orow2[0:1, :], osq[0:1, :], ALU.subtract))
                nc.vector.wait_ge(dv.h, n3)
                n3b = dv.inc(nc.vector.tensor_scalar(orow2[0:1, :], orow2[0:1, :], LN_EPS, None, ALU.add))
                nc.scalar.wait_ge(dv.h, n3b)
                n4 = se.inc(nc.scalar.activation(orow2[0:1, :], orow2[0:1, :], AF.Sqrt))
                nc.vector.wait_ge(se.h, n4)
                n5 = dv.inc(nc.vector.reciprocal(orow2[0:1, :], orow2[0:1, :]))
                for q in range(NQ):
                    nc.tensor.wait_ge(dv.h, max(n5, dv.n))
                    nc.tensor.wait_ge(se.h, se.n)
                    nc.tensor.matmul(pmm[:], self.c_onesf[0:1, :],
                                     orow[0:1, q * 512: q * 512 + 512],
                                     start=True, stop=True)
                    nmm = pe.inc(nc.tensor.matmul(pmm2[:], self.c_onesf[0:1, :],
                                                  orow2[0:1, q * 512: q * 512 + 512],
                                                  start=True, stop=True))
                    nc.vector.wait_ge(pe.h, nmm)
                    nc.vector.tensor_copy(obc[:, q * 512:q * 512 + 512], pmm[:])
                    dv.inc(nc.vector.tensor_copy(obc2[:, q * 512:q * 512 + 512], pmm2[:]))
                nc.vector.wait_ge(dv.h, dv.n)
                nc.vector.wait_ge(se.h, nsq)
                nc.vector.tensor_tensor(osq[:], o_cm[:, hh * SEG:(hh + 1) * SEG], obc[:], ALU.subtract)
                nc.vector.tensor_tensor(osq[:], osq[:], obc2[:], ALU.mult)
                nc.vector.tensor_scalar(osq[:], osq[:], self.c_lng[:, 0:1], self.c_lnb[:, 0:1], ALU.mult, ALU.add)
                nst = dv.inc(nc.vector.tensor_tensor(oh_seg[:, hh * SEG:(hh + 1) * SEG], osq[:],
                                                     g_cm[:, hh * SEG:(hh + 1) * SEG], ALU.mult))
                nc.scalar.wait_ge(dv.h, nst)
                st.inc(nc.scalar.dma_start(out=self.d_oh_in[hh * 128:(hh + 1) * 128, t0:t0 + SEG],
                                           in_=oh_seg[:, hh * SEG:(hh + 1) * SEG]), 16)
        self.p2_store_n = st.n

    def emit_gather2(self):
        nc = self.nc
        cc, st, gp = self.s_cc, self.s_st, self.s_gp
        nc.gpsimd.wait_ge(st.h, self.p2_store_n)
        cc.inc(nc.gpsimd.collective_compute("AllGather", ALU.bypass, [list(range(NCORE))],
                                            [self.d_oh_in.opt()], [self.d_oh_all.opt()]), 1)
        self.g2_cc = cc.n
        if "ohat" in self.dbgt:
            gp.inc(nc.gpsimd.dma_start(out=self.dbgt["ohat"], in_=self.d_oh_in), 16)

    def emit_phase3a(self):
        nc = self.nc
        T, TSL = self.T, self.TSL
        NTT = TSL // 128
        ld, pe, dv, se, st, gp, w, cc = (self.s_ld, self.s_pe, self.s_dv, self.s_se,
                                         self.s_st, self.s_gp, self.s_w, self.s_cc)
        sb = lambda n, s_, d_: self.sb(n, s_, d_, self.ph3a)

        oh_sl = sb("p3_oh", [128, 16 * TSL], BF16)
        i_oh_t = sb("p3_ioh", [128, 16], U32)
        nc.sync.dma_start(out=i_oh_t[:], in_=self.i_oh.rearrange("(p a) -> p a", p=128)).then_inc(ld.h, 16); ld.n += 16
        nc.gpsimd.wait_ge(cc.h, self.g2_cc)
        nc.gpsimd.wait_ge(ld.h, ld.n)
        for cb in range(16):
            ins = nc.gpsimd.indirect_dma_start(
                out=oh_sl[:, cb * TSL:(cb + 1) * TSL], out_offset=None,
                in_=self.d_oh_all.rearrange("c (b t) -> (c b) t", b=NCORE),
                in_offset=IndirectOffsetOnAxis(ap=i_oh_t[:, cb:cb + 1], axis=0))
            gp.inc(ins, 16)
        n_oh = gp.n

        wo_buf = sb("p3_wob", [128, 16 * 512], BF16)
        xres = sb("p3_xres", [128, 512], F32)
        x1t = sb("p3_x1t", [128, 512], F32)
        self.pA = self.ps("p3_pA", [128, 512], F32, self.ph3)
        self.pB = self.ps("p3_pB", [128, 512], F32, self.ph3)
        pA, pB = self.pA, self.pB
        pmm = pA
        pmm2 = pB
        evac_n = {}
        for ocq in range(4):
            nc.gpsimd.wait_ge(pe.h, pe.n)
            for icb in range(16):
                ins = nc.gpsimd.dma_start(out=wo_buf[:, icb * 512:(icb + 1) * 512],
                                          in_=self.wo[icb * 128:(icb + 1) * 128, ocq * 512:(ocq + 1) * 512])
                w.inc(ins, 16)
            n_wo = w.n
            for tt in range(NTT):
                pp = pmm if (tt % 2 == 0) else pmm2
                nc.tensor.wait_ge(w.h, n_wo)
                nc.tensor.wait_ge(gp.h, n_oh)
                nc.tensor.wait_ge(dv.h, evac_n.get(id(pp), 0))
                for icb in range(16):
                    ins = nc.tensor.matmul(pp[:], oh_sl[:, icb * TSL + tt * 128: icb * TSL + tt * 128 + 128],
                                           wo_buf[:, icb * 512:(icb + 1) * 512],
                                           start=(icb == 0), stop=(icb == 15))
                nmm = pe.inc(ins)
                nc.sync.wait_ge(dv.h, dv.n)
                nc.sync.dma_start(out=xres[:], in_=self.x_sl[tt * 128:(tt + 1) * 128,
                                                             ocq * 512:(ocq + 1) * 512]).then_inc(ld.h, 16)
                ld.n += 16
                nc.vector.wait_ge(pe.h, nmm)
                nc.vector.wait_ge(ld.h, ld.n)
                nev = dv.inc(nc.vector.tensor_tensor(x1t[:], pp[:], xres[:], ALU.add))
                evac_n[id(pp)] = nev
                nc.scalar.wait_ge(dv.h, nev)
                st.inc(nc.scalar.dma_start(out=self.d_x1[tt * 128:(tt + 1) * 128, ocq * 512:(ocq + 1) * 512],
                                           in_=x1t[:]), 16)
        n_x1 = st.n
        if "x1" in self.dbgt:
            nc.gpsimd.wait_ge(st.h, n_x1)
            gp.inc(nc.gpsimd.dma_start(out=self.dbgt["x1"], in_=self.d_x1), 16)
        self.barrier()

        xti = sb("p3_x", [128, DM], F32)
        xc = sb("p3_xc", [128, DM], F32)
        xsq = sb("p3_xsq", [128, DM], F32)
        stat = sb("p3_stat", [128, 4], F32)
        xh = sb("p3_xh", [128, DM], BF16)
        xhT = sb("p3_xhT", [128, 16 * 128], BF16)
        w8all = sb("p3_w8", [128, NTT * 8], F32)
        s8 = sb("p3_s8", [128, 8], F32)
        sb8 = sb("p3_sb8", [128, 8], F32)
        m8 = sb("p3_m8", [128, 8], F32)
        gwb = sb("p3_gwb", [128, 16 * 8], BF16)
        w8T = sb("p3_w8T", [8, TSL], F32)
        ptb = self.ps("p3_ptb", [128, 128], BF16, self.ph3a)
        ptf = self.ps("p3_ptf", [128, 128], F32, self.ph3a)
        pg8 = self.ps("p3_pg8", [128, 8], F32, self.ph3a)

        gp.inc(nc.gpsimd.dma_start(out=gwb[:].rearrange("p (a e) -> p a e", a=16), in_=self.gate_wT.rearrange("(a p) e -> p a e", p=128)), 16)
        n_gw = gp.n

        for tt in range(NTT):
            nc.sync.wait_ge(st.h, st.n)
            nc.sync.wait_ge(dv.h, dv.n)
            nc.sync.dma_start(out=xti[:], in_=self.d_x1[tt * 128:(tt + 1) * 128, :]).then_inc(ld.h, 16); ld.n += 16
            nc.vector.wait_ge(ld.h, ld.n)
            nxh = self.emit_ln_tile(xti, xc, xsq, stat, xh)
            for cb in range(16):
                nc.tensor.wait_ge(dv.h, nxh if cb == 0 else dv.n)
                nmm = pe.inc(nc.tensor.transpose(ptb[:], xh[:, cb * 128:(cb + 1) * 128], self.c_ident_bf[:]))
                nc.vector.wait_ge(pe.h, nmm)
                dv.inc(nc.vector.tensor_copy(xhT[:, cb * 128:(cb + 1) * 128], ptb[:]))
            ntr = dv.n
            nc.scalar.wait_ge(dv.h, ntr)
            for cb in range(16):
                st.inc(nc.scalar.dma_start(out=self.d_x2T[cb * 128:(cb + 1) * 128, tt * 128:(tt + 1) * 128],
                                           in_=xhT[:, cb * 128:(cb + 1) * 128]), 16)
            nc.tensor.wait_ge(dv.h, ntr)
            nc.tensor.wait_ge(gp.h, n_gw)
            nc.tensor.wait_ge(se.h, se.n)
            for cb in range(16):
                ins = nc.tensor.matmul(pg8[:], xhT[:, cb * 128:(cb + 1) * 128],
                                       gwb[:, cb * 8:(cb + 1) * 8], start=(cb == 0), stop=(cb == 15))
            ngm = pe.inc(ins)
            nc.scalar.wait_ge(pe.h, ngm)
            nsg = se.inc(nc.scalar.activation(s8[:], pg8[:], AF.Sigmoid))
            nc.vector.wait_ge(se.h, nsg)
            n = dv.inc(nc.vector.tensor_tensor(sb8[:], s8[:], self.c_gateb[:], ALU.add))
            nc.vector.wait_ge(dv.h, n)
            n = dv.inc(nc.vector.tensor_reduce(stat[:, 0:1], sb8[:], AX.X, ALU.max))
            nc.vector.wait_ge(dv.h, n)
            n = dv.inc(nc.vector.tensor_scalar(m8[:], sb8[:], stat[:, 0:1], None, ALU.is_ge))
            nc.vector.wait_ge(dv.h, n)
            n = dv.inc(nc.vector.scalar_tensor_tensor(sb8[:], m8[:], -BIGNEG, sb8[:], ALU.mult, ALU.add))
            nc.vector.wait_ge(dv.h, n)
            n = dv.inc(nc.vector.tensor_reduce(stat[:, 1:2], sb8[:], AX.X, ALU.max))
            nc.vector.wait_ge(dv.h, n)
            n = dv.inc(nc.vector.tensor_scalar(sb8[:], sb8[:], stat[:, 1:2], None, ALU.is_ge))
            nc.vector.wait_ge(dv.h, n)
            n = dv.inc(nc.vector.tensor_tensor(m8[:], m8[:], sb8[:], ALU.add))
            nc.vector.wait_ge(dv.h, n)
            n = dv.inc(nc.vector.tensor_tensor(m8[:], m8[:], s8[:], ALU.mult))
            nc.vector.wait_ge(dv.h, n)
            n = dv.inc(nc.vector.tensor_reduce(stat[:, 2:3], m8[:], AX.X, ALU.add))
            nc.vector.wait_ge(dv.h, n)
            n = dv.inc(nc.vector.reciprocal(stat[:, 3:4], stat[:, 2:3]))
            nc.vector.wait_ge(dv.h, n)
            nw8 = dv.inc(nc.vector.tensor_scalar(w8all[:, tt * 8:(tt + 1) * 8], m8[:], stat[:, 3:4], None, ALU.mult))
            nc.tensor.wait_ge(dv.h, nw8)
            nmm = pe.inc(nc.tensor.transpose(ptf[0:8, :], w8all[:, tt * 8:(tt + 1) * 8], self.c_ident_f32[:]))
            nc.vector.wait_ge(pe.h, nmm)
            dv.inc(nc.vector.tensor_copy(w8T[:, tt * 128:(tt + 1) * 128], ptf[0:8, :]))
        n_gates = dv.n
        nc.scalar.wait_ge(dv.h, n_gates)
        st.inc(nc.scalar.dma_start(out=self.d_w8T, in_=w8T[:]), 16)
        if "gates" in self.dbgt:
            nc.scalar.wait_ge(dv.h, n_gates)
            for tt in range(NTT):
                st.inc(nc.scalar.dma_start(out=self.dbgt["gates"][tt * 128:(tt + 1) * 128, :],
                                           in_=w8all[:, tt * 8:(tt + 1) * 8]), 16)
        if "x2T" in self.dbgt:
            nc.gpsimd.wait_ge(st.h, st.n)
            gp.inc(nc.gpsimd.dma_start(out=self.dbgt["x2T"], in_=self.d_x2T), 16)

    def emit_phase3b(self):
        nc = self.nc
        T, TSL = self.T, self.TSL
        NTT = TSL // 128
        ld, pe, dv, se, st, gp, w, cc = (self.s_ld, self.s_pe, self.s_dv, self.s_se,
                                         self.s_st, self.s_gp, self.s_w, self.s_cc)
        sb = lambda n, s_, d_: self.sb(n, s_, d_, self.ph3)
        pA, pB = self.pA, self.pB
        evac_n = {}

        NH = max(1, TSL // 1024)
        HALF = TSL // NH
        NTTH = HALF // 128
        NQH = HALF // 512
        x2h = sb("p3_x2h", [128, 16 * HALF], BF16)
        harena = sb("p3_h", [128, 22 * HALF], BF16)
        sgt = sb("p3_sgt", [128, HALF], BF16)
        upbuf0 = sb("p3_up0", [128, 16 * 128], BF16)
        upbuf1 = sb("p3_up1", [128, 16 * 128], BF16)
        dwbuf = sb("p3_dw", [128, 22 * 512], BF16)
        webc = sb("p3_webc", [128, HALF], F32)
        iwe_t = sb("p3_iwe", [128, 1], U32)
        acc = sb("p3_acc", [128, NTTH * DM], F32)
        xf1 = sb("p3_xf1", [128, 512], F32)
        pup = pA
        pup2 = pB
        pdn = pA
        pdn2 = pB
        ub_last = {0: 0, 1: 0}

        for half in range(NH):
            h0 = half * HALF
            self.barrier()
            for icb in range(16):
                nc.sync.dma_start(out=x2h[:, icb * HALF:(icb + 1) * HALF],
                                  in_=self.d_x2T[icb * 128:(icb + 1) * 128, h0:h0 + HALF]).then_inc(ld.h, 16)
                ld.n += 16
            n_x2 = ld.n

            experts = ["sh"] + list(range(NE))
            prev_n_h = 0
            for e in experts:
                n_up_oc = 44 if e == "sh" else 22
                n_hs = n_up_oc // 2
                upw = self.shared_gwT if e == "sh" else self.up_wT[e]
                if e != "sh":
                    nc.sync.wait_ge(st.h, st.n)
                    nc.sync.wait_ge(dv.h, dv.n)
                    nc.sync.dma_start(out=iwe_t[:],
                                      in_=self.i_we[e * NH + half: e * NH + half + 1]
                                      .unsqueeze(0).partition_broadcast(128).squeeze(1)).then_inc(ld.h, 16)
                    ld.n += 16
                    nc.gpsimd.wait_ge(ld.h, ld.n)
                    nwe = gp.inc(nc.gpsimd.indirect_dma_start(
                        out=webc[:], out_offset=None,
                        in_=self.d_w8T.rearrange("e (nh t) -> (e nh) t", nh=NH),
                        in_offset=IndirectOffsetOnAxis(ap=iwe_t[:, 0:1], axis=0)), 16)
                n_webc = gp.n

                nw_pair = [0, 0]
                for pair in range(n_hs):
                    for side, ocb in enumerate([pair, pair + n_hs]):
                        ub = upbuf0 if side == 0 else upbuf1
                        nc.gpsimd.wait_ge(pe.h, ub_last[side])
                        ins = nc.gpsimd.dma_start(
                            out=ub[:].rearrange("p (a o) -> p a o", a=16),
                            in_=upw[:, ocb * 128:(ocb + 1) * 128].rearrange("(a p) o -> p a o", p=128))
                        nw_pair[side] = w.inc(ins, 16)
                    for qq in range(NQH):
                        flip = (pair * NQH + qq) % 2 == 0
                        ppg = pA if flip else pB
                        ppu = pB if flip else pA
                        hsl = slice(pair * HALF + qq * 512, pair * HALF + qq * 512 + 512)
                        ssl = slice(qq * 512, qq * 512 + 512)
                        nc.tensor.wait_ge(w.h, nw_pair[0])
                        nc.tensor.wait_ge(ld.h, n_x2)
                        nc.tensor.wait_ge(dv.h, evac_n.get(id(ppg), 0))
                        nc.tensor.wait_ge(se.h, evac_n.get("se" + str(id(ppg)), 0))
                        for icb in range(16):
                            ins = nc.tensor.matmul(ppg[:], upbuf0[:, icb * 128:(icb + 1) * 128],
                                                   x2h[:, icb * HALF + qq * 512: icb * HALF + qq * 512 + 512],
                                                   start=(icb == 0), stop=(icb == 15))
                        nmmg = pe.inc(ins)
                        nc.tensor.wait_ge(w.h, nw_pair[1])
                        nc.tensor.wait_ge(dv.h, evac_n.get(id(ppu), 0))
                        nc.tensor.wait_ge(se.h, evac_n.get("se" + str(id(ppu)), 0))
                        for icb in range(16):
                            ins = nc.tensor.matmul(ppu[:], upbuf1[:, icb * 128:(icb + 1) * 128],
                                                   x2h[:, icb * HALF + qq * 512: icb * HALF + qq * 512 + 512],
                                                   start=(icb == 0), stop=(icb == 15))
                        nmmu = pe.inc(ins)
                        if e != "sh":
                            nc.scalar.wait_ge(pe.h, nmmg)
                            nc.scalar.wait_ge(dv.h, dv.n)
                            nse = se.inc(nc.scalar.activation(sgt[:, ssl], ppg[:], AF.Silu))
                            evac_n["se" + str(id(ppg))] = nse
                            nc.vector.wait_ge(se.h, nse)
                            nc.vector.wait_ge(pe.h, nmmu)
                            nc.vector.wait_ge(gp.h, n_webc)
                            nev = dv.inc(nc.vector.tensor_tensor(harena[:, hsl], sgt[:, ssl], ppu[:], ALU.mult))
                            nc.vector.wait_ge(dv.h, nev)
                            evac_n[id(ppu)] = dv.inc(nc.vector.tensor_tensor(
                                harena[:, hsl], harena[:, hsl], webc[:, ssl], ALU.mult))
                        else:
                            nc.scalar.wait_ge(pe.h, nmmg)
                            nc.scalar.wait_ge(dv.h, dv.n)
                            ncp = se.inc(nc.scalar.copy(sgt[:, ssl], ppg[:]))
                            evac_n["se" + str(id(ppg))] = ncp
                            nc.scalar.wait_ge(pe.h, nmmu)
                            nsi = se.inc(nc.scalar.activation(harena[:, hsl], ppu[:], AF.Silu))
                            evac_n["se" + str(id(ppu))] = nsi
                            nc.vector.wait_ge(se.h, nsi)
                            evac_n[id(ppu)] = dv.inc(nc.vector.tensor_tensor(
                                harena[:, hsl], harena[:, hsl], sgt[:, ssl], ALU.mult))
                    ub_last[0] = ub_last[1] = pe.n
                n_h = dv.n

                dw = self.shared_dwT if e == "sh" else self.down_wT[e]
                for cq in range(4):
                    nc.gpsimd.wait_ge(pe.h, pe.n)
                    ins = nc.gpsimd.dma_start(
                        out=dwbuf[:, 0:n_hs * 512].rearrange("p (a o) -> p a o", a=n_hs),
                        in_=dw[:, cq * 512:(cq + 1) * 512].rearrange("(a p) o -> p a o", p=128))
                    w.inc(ins, 16)
                    n_w = w.n
                    for tt in range(NTTH):
                        pp = pdn if (tt % 2 == 0) else pdn2
                        nc.tensor.wait_ge(w.h, n_w)
                        nc.tensor.wait_ge(dv.h, max(n_h, evac_n.get(id(pp), 0)))
                        for hsb in range(n_hs):
                            ins = nc.tensor.matmul(pp[:],
                                                   harena[:, hsb * HALF + tt * 128: hsb * HALF + tt * 128 + 128],
                                                   dwbuf[:, hsb * 512:(hsb + 1) * 512],
                                                   start=(hsb == 0), stop=(hsb == n_hs - 1))
                        nmm = pe.inc(ins)
                        nc.vector.wait_ge(pe.h, nmm)
                        aslice = acc[:, tt * DM + cq * 512: tt * DM + cq * 512 + 512]
                        if e == "sh":
                            ins = nc.vector.tensor_copy(aslice, pp[:])
                        else:
                            ins = nc.vector.tensor_tensor(aslice, aslice, pp[:], ALU.add)
                        evac_n[id(pp)] = dv.inc(ins)
            n_ffn = dv.n

            for tt in range(NTTH):
                gtt = half * NTTH + tt
                for cq in range(4):
                    nc.sync.wait_ge(dv.h, dv.n)
                    nc.sync.dma_start(out=xf1[:], in_=self.d_x1[gtt * 128:(gtt + 1) * 128,
                                                               cq * 512:(cq + 1) * 512]).then_inc(ld.h, 16)
                    ld.n += 16
                    nc.vector.wait_ge(ld.h, ld.n)
                    nc.vector.wait_ge(dv.h, n_ffn)
                    nev = dv.inc(nc.vector.tensor_tensor(xf1[:], acc[:, tt * DM + cq * 512: tt * DM + cq * 512 + 512],
                                                         xf1[:], ALU.add))
                    nc.scalar.wait_ge(dv.h, nev)
                    st.inc(nc.scalar.dma_start(out=self.out_sl[gtt * 128:(gtt + 1) * 128, cq * 512:(cq + 1) * 512],
                                               in_=xf1[:]), 16)
            if "acc" in self.dbgt:
                nc.scalar.wait_ge(dv.h, n_ffn)
                for tt in range(NTTH):
                    st.inc(nc.scalar.dma_start(
                        out=self.dbgt["acc"][(half * NTTH + tt) * 128:(half * NTTH + tt + 1) * 128, :],
                        in_=acc[:, tt * DM:(tt + 1) * DM]), 16)

        nc.sync.wait_ge(st.h, st.n)
        nc.gpsimd.wait_ge(st.h, st.n)


def host_inputs(inp, T=16384):
    import ml_dtypes
    TSL = T // NCORE
    SEGc = min(1024, TSL)
    NSEG = T // SEGc
    f32 = np.float32
    x = np.asarray(inp["L0_attn_x"], f32).reshape(T, DM)
    ident = np.eye(128)
    cm = np.zeros((L, L), f32)
    dq = np.zeros((L, L), f32)
    for s_ in range(L):
        for t_ in range(L):
            if s_ < t_:
                cm[s_, t_] = SCALE * SCALE
            if s_ <= t_:
                dq[s_, t_] = 1.0
    cmask = np.concatenate([cm, cm], axis=1)
    dqmask = np.concatenate([dq, dq], axis=1)
    bf = ml_dtypes.bfloat16
    maps = []
    for r in range(NCORE):
        hsl = slice(r * HPC * D, (r + 1) * HPC * D)
        i_beta = np.array([rr * 16 + r * HPC + hh for hh in range(HPC) for rr in range(NCORE)], np.uint32)
        i_bbc = np.array([((seg * SEGc // TSL) * 16 + r * HPC + hh) * (TSL // SEGc) + (seg * SEGc % TSL) // SEGc
                          for hh in range(HPC) for seg in range(NSEG)], np.uint32)
        i_oh = np.array([(a * 128 + p) * NCORE + r for p in range(128) for a in range(16)], np.uint32)
        NHh = max(1, TSL // 1024)
        i_we = np.zeros(64, np.uint32)
        for e in range(8):
            for hf in range(NHh):
                i_we[e * NHh + hf] = e * NHh + hf
        m = {
            "x_sl": np.ascontiguousarray(x[r * TSL:(r + 1) * TSL]),
            "wq_s": np.ascontiguousarray(np.asarray(inp["L0_attn_Wq"], f32)[:, hsl]),
            "wk_s": np.ascontiguousarray(np.asarray(inp["L0_attn_Wk"], f32)[:, hsl]),
            "wv_s": np.ascontiguousarray(np.asarray(inp["L0_attn_Wv"], f32)[:, hsl]),
            "wa_s": np.ascontiguousarray(np.asarray(inp["L0_attn_Wa"], f32)[:, hsl]),
            "wg_s": np.ascontiguousarray(np.asarray(inp["L0_attn_Wg"], f32)[:, hsl]),
            "ba_s": np.ascontiguousarray(np.asarray(inp["L0_attn_ba"], f32)[hsl]),
            "wb_full": np.ascontiguousarray(np.asarray(inp["L0_attn_Wb"], f32)),
            "bb": np.ascontiguousarray(np.asarray(inp["L0_attn_bb"], f32)),
            "qcw_s": np.ascontiguousarray(np.asarray(inp["L0_attn_qconv_w"], f32)[hsl]),
            "kcw_s": np.ascontiguousarray(np.asarray(inp["L0_attn_kconv_w"], f32)[hsl]),
            "vcw_s": np.ascontiguousarray(np.asarray(inp["L0_attn_vconv_w"], f32)[hsl]),
            "qcb_s": np.ascontiguousarray(np.asarray(inp["L0_attn_qconv_b"], f32)[hsl]),
            "kcb_s": np.ascontiguousarray(np.asarray(inp["L0_attn_kconv_b"], f32)[hsl]),
            "vcb_s": np.ascontiguousarray(np.asarray(inp["L0_attn_vconv_b"], f32)[hsl]),
            "ln_g": np.ascontiguousarray(np.asarray(inp["L0_attn_ln_g"], f32)),
            "ln_b": np.ascontiguousarray(np.asarray(inp["L0_attn_ln_b"], f32)),
            "wo": np.ascontiguousarray(np.asarray(inp["L0_attn_Wo"], f32)),
            "gate_wT": np.ascontiguousarray(np.asarray(inp["L0_ffn_gate_w"], f32).T),
            "gate_b": np.ascontiguousarray(np.asarray(inp["L0_ffn_gate_bias"], f32)),
            "shared_gwT": np.ascontiguousarray(np.asarray(inp["L0_ffn_shared_gw"], f32).T),
            "shared_dwT": np.ascontiguousarray(np.asarray(inp["L0_ffn_shared_dw"], f32).T),
            "up_wT": np.ascontiguousarray(np.asarray(inp["L0_ffn_up_w"], f32).transpose(0, 2, 1)),
            "down_wT": np.ascontiguousarray(np.asarray(inp["L0_ffn_down_w"], f32).transpose(0, 2, 1)),
            "ident_bf": ident.astype(bf),
            "ident_f32": ident.astype(f32),
            "cmask": cmask.astype(bf),
            "dqmask": dqmask.astype(bf),
            "i_beta": i_beta,
            "i_bbc": i_bbc,
            "i_oh": i_oh,
            "i_we": i_we,
        }
        maps.append(m)
    return maps


from concourse.bass_utils import run_bass_kernel_spmd

_CACHED = {}

def kernel(**inputs):
    T = int(np.asarray(inputs["L0_attn_x"]).shape[1])
    if T not in _CACHED:
        b = Builder(T=T)
        _CACHED[T] = b.build()
    nc = _CACHED[T]
    maps = host_inputs(inputs, T=T)
    res = run_bass_kernel_spmd(nc, maps, core_ids=list(range(NCORE)))
    out = np.concatenate([res.results[r]["out_sl"] for r in range(NCORE)], axis=0)
    return out.reshape(1, T, DM).astype(np.float32)
